# version 01
# baseline (speedup 1.0000x reference)
import contextlib
import numpy as np
import concourse.bass as bass
import concourse.mybir as mybir
from concourse.bass_utils import run_bass_kernel_spmd
from concourse.bass_core_rust import IndirectOffsetOnAxis

F32 = mybir.dt.float32
U32 = mybir.dt.uint32
I32 = mybir.dt.int32
AF = mybir.ActivationFunctionType
ALU = mybir.AluOpType
AX = mybir.AxisListType

B, N, D, H, DH = 8, 4096, 1024, 16, 64
NT, DC = N // 128, D // 128
KCL, FC = 4, 8
TK, NSUB = 38, 5
GA, GB = NSUB * TK, (FC - NSUB) * TK
G = KCL * (GA + GB)
GP, GT = 1280, 10
CAP, E, HF = 512, 8, 4096
SCALE = DH ** -0.5
EPS = 1e-5
ARENA_COLS = 52992


def _cluster_chunks():
    out = []
    for ci in range(KCL):
        need = np.zeros(GP, bool)
        need[0:4 * GA + GB] = True
        if ci > 0:
            s = 4 * GA + GB * ci
            need[s:s + GB] = True
        chunks = []
        for c in range(GT):
            seg = need[c * 128:(c + 1) * 128]
            if seg.any():
                chunks.append((c, None if seg.all() else seg.astype(np.float32)))
        out.append(chunks)
    return out


CLUSTER_CHUNKS = _cluster_chunks()
N_MASKS = sum(1 for ch in CLUSTER_CHUNKS for (_, m) in ch if m is not None)


FCONSTS = [EPS, 1.0 / D, SCALE]


def _const_layout():
    off, o = {}, 0
    for name, n in (("ident", 128), ("ones_row", 128), ("masks", max(N_MASKS, 1)),
                    ("alt", NT), ("tokid", NT), ("altr", NT), ("basenc", 1),
                    ("prefill", 64), ("fconst", len(FCONSTS))):
        off[name] = o
        o += n
    return off, o


CONST_OFF, CONST_COLS = _const_layout()


def _make_consts():
    c = np.zeros((128, CONST_COLS), np.float32)
    o = CONST_OFF
    c[:, o["ident"]:o["ident"] + 128] = np.eye(128, dtype=np.float32)
    c[0, o["ones_row"]:o["ones_row"] + 128] = 1.0
    mi = 0
    for ch in CLUSTER_CHUNKS:
        for (_, m) in ch:
            if m is not None:
                c[:, o["masks"] + mi] = m
                mi += 1
    tok = (np.arange(128)[:, None] + 128 * np.arange(NT)[None, :]).astype(np.float32)
    c[:, o["tokid"]:o["tokid"] + NT] = tok
    c[:, o["alt"]:o["alt"] + NT] = tok + 4096.0
    rowid = (32 * np.arange(128)[:, None] + np.arange(NT)[None, :]).astype(np.float32)
    c[:, o["altr"]:o["altr"] + NT] = rowid + 4096.0
    c[0:4, o["basenc"]] = np.arange(4, dtype=np.float32) * 1024.0
    pf = np.zeros((128, 64), np.float32)
    pf[:, 0::2] = -1.0
    c[:, o["prefill"]:o["prefill"] + 64] = pf
    for i, v in enumerate(FCONSTS):
        c[:, o["fconst"] + i] = np.float32(v)
    return c


MASK_COL = {}
_mi = 0
for _ci, _ch in enumerate(CLUSTER_CHUNKS):
    for (_c, _m) in _ch:
        if _m is not None:
            MASK_COL[(_ci, _c)] = _mi
            _mi += 1


class Sem:
    def __init__(self, stack, nc, name):
        self.h = stack.enter_context(nc.semaphore(name))
        self.n = 0

    def inc(self, k=1):
        self.n += k
        return self.n


class Buf:
    __slots__ = ("name", "ready", "readers", "dsem_hw", "dsem_sw")

    def __init__(self, name=""):
        self.name = name
        self.ready = None
        self.readers = {}
        self.dsem_hw = None
        self.dsem_sw = None


class Arena:
    def __init__(self):
        self.off = 0
        self.marks = []

    def alloc(self, cols, name=""):
        o = (self.off + 15) // 16 * 16
        assert o + cols <= ARENA_COLS, f"arena overflow at {name}: {o}+{cols}"
        self.off = o + cols
        return o

    def mark(self):
        self.marks.append(self.off)

    def reset(self):
        self.off = self.marks.pop()


class Builder:

    ENGS = ("sync", "tensor", "vector", "scalar", "gpsimd")

    def __init__(self, nc, stack):
        self.nc = nc
        self.stack = stack
        self.streams = {e: [] for e in self.ENGS}
        self.comp = {e: Sem(stack, nc, f"c_{e}") for e in self.ENGS}
        self.dsems = []

    def _dsem(self, buf, sw):
        attr = "dsem_sw" if sw else "dsem_hw"
        s = getattr(buf, attr)
        if s is None:
            s = Sem(self.stack, self.nc, f"ds{len(self.dsems)}")
            setattr(buf, attr, s)
            self.dsems.append(s)
        return s

    def emit(self, eng, fns, reads=(), writes=(), dma=False):
        if callable(fns):
            fns = [fns]
        if dma:
            assert len(writes) >= 1
            sem = self._dsem(writes[0], sw=(eng == "gpsimd"))
        else:
            sem = self.comp[eng]
        waits = {}

        def add(sv):
            if sv is None:
                return
            s, v = sv
            if waits.get(s, 0) < v:
                waits[s] = v

        for b in reads:
            add(b.ready)
        for b in writes:
            if not (dma and b.ready is not None and b.ready[0] is sem
                    and not b.readers):
                add(b.ready)
            for s, v in b.readers.items():
                add((s, v))
        k = 16 if dma else 1
        tgt = sem.inc(k)
        for b in writes:
            b.ready = (sem, tgt)
            b.readers = {}
        for b in reads:
            b.readers[sem] = tgt
        wl = list(waits.items())

        def run(e, wl=wl, fns=fns, sem=sem, k=k):
            for s, v in wl:
                e.wait_ge(s.h, v)
            ins = None
            for f in fns:
                ins = f(e)
            ins.then_inc(sem.h, k)

        self.streams[eng].append(run)

    def barrier(self):
        targets = ([(s, s.n) for s in self.comp.values() if s.n > 0]
                   + [(s, s.n) for s in self.dsems if s.n > 0])
        for e in self.ENGS:
            own = self.comp[e]
            wl = [(s, v) for s, v in targets if s is not own]

            def run(eng, wl=wl):
                for s, v in wl:
                    eng.wait_ge(s.h, v)

            self.streams[e].append(run)

    def final_drain(self, eng):
        sems = [(s, s.n) for s in self.dsems]

        def run(e, sems=sems):
            for s, v in sems:
                e.wait_ge(s.h, v)

        self.streams[eng].append(run)


DBG_TAPS = set()
PHASES = 99
NEXPERTS = E
P7STEPS = 5
FFN1_MODE = 0


def build_program(dbg=False):
    nc = bass.Bass("TRN2", target_bir_lowering=False)
    stack = contextlib.ExitStack()
    scratch_kind = "ExternalOutput" if dbg else "Internal"

    dt = {}
    dt["x"] = nc.dram_tensor("x", [N, D], F32, kind="ExternalInput").ap()
    dt["kfx"] = nc.dram_tensor("kfx", [B * 512, D], F32, kind="ExternalInput").ap()
    dt["wqkv"] = nc.dram_tensor("wqkv", [D, 3 * D], F32, kind="ExternalInput").ap()
    dt["bqkv"] = nc.dram_tensor("bqkv", [3 * D], F32, kind="ExternalInput").ap()
    dt["wproj"] = nc.dram_tensor("wproj", [D, D], F32, kind="ExternalInput").ap()
    dt["bproj"] = nc.dram_tensor("bproj", [D], F32, kind="ExternalInput").ap()
    dt["wg"] = nc.dram_tensor("wg", [D, E], F32, kind="ExternalInput").ap()
    dt["w1c"] = nc.dram_tensor("w1c", [E, 32, 128, DC * 128], F32,
                               kind="ExternalInput").ap()
    dt["w2"] = nc.dram_tensor("w2", [E, HF, D], F32, kind="ExternalInput").ap()
    dt["consts"] = nc.dram_tensor("consts", [128, CONST_COLS], F32,
                                  kind="ExternalInput").ap()
    dt["out"] = nc.dram_tensor("out", [N, D], F32, kind="ExternalOutput").ap()
    dt["x1s"] = nc.dram_tensor("x1s", [2 * N, D], F32, kind=scratch_kind).ap()
    dt["gdram"] = nc.dram_tensor("gdram", [GP], F32, kind=scratch_kind).ap()
    dt["slots"] = nc.dram_tensor("slots", [2 * N, 2], F32, kind=scratch_kind).ap()

    arena_t = stack.enter_context(nc.sbuf_tensor("arena", [128, ARENA_COLS], F32))
    AR = arena_t[:, :]
    psum_t = [stack.enter_context(nc.psum_tensor(f"ps{i}", [128, 1024], F32))
              for i in range(4)]
    PS = [p[:, :] for p in psum_t]
    ps_buf = [Buf(f"ps{i}{h}") for i in range(4) for h in (0, 1)]

    def psv(i, half=None, cols=None):
        if cols is not None:
            return PS[i][:, cols[0]:cols[1]]
        return PS[i][:, half * 512:(half + 1) * 512]

    def psb(i, half):
        return ps_buf[2 * i + half]

    bld = Builder(nc, stack)
    ar = Arena()
    em = bld.emit
    TS, VS, SS = bld.comp["tensor"], bld.comp["vector"], bld.comp["scalar"]
    CO = CONST_OFF

    o_consts = ar.alloc(CONST_COLS, "consts")
    o_bqkvb = ar.alloc(3 * D, "bqkv_bcast")
    o_bprojb = ar.alloc(D, "bproj_bcast")
    o_bqs = ar.alloc(16, "bq_bk_cols")
    o_small = ar.alloc(128, "smalls")
    o_lnt = ar.alloc(16, "ln_tmp")
    o_lns = ar.alloc(1024, "ln_scratch")
    b_consts, b_bqkvb, b_bprojb, b_bqs, b_small = (
        Buf(x) for x in ("consts", "bqkvb", "bprojb", "bqs", "small"))

    CONSTS = AR[:, o_consts:o_consts + CONST_COLS]
    IDENT = CONSTS[:, CO["ident"]:CO["ident"] + 128]
    ONES_ROW = AR[0:1, o_consts + CO["ones_row"]:o_consts + CO["ones_row"] + 128]
    TOKID = CONSTS[:, CO["tokid"]:CO["tokid"] + NT]
    ALT = CONSTS[:, CO["alt"]:CO["alt"] + NT]
    ALTR = CONSTS[:, CO["altr"]:CO["altr"] + NT]
    BASENC = AR[0:4, o_consts + CO["basenc"]:o_consts + CO["basenc"] + 1]
    PREFILL = CONSTS[:, CO["prefill"]:CO["prefill"] + 64]
    BQKVB = AR[:, o_bqkvb:o_bqkvb + 3 * D]
    BPROJB = AR[:, o_bprojb:o_bprojb + D]
    BQC = AR[:, o_bqs:o_bqs + 8]
    BKC = AR[:, o_bqs + 8:o_bqs + 16]
    LNS = AR[:, o_lns:o_lns + 1024]
    GSBF = AR[:, o_small:o_small + GT]
    GSBU = AR[:, o_small + 16:o_small + 16 + GT]

    for i, v in enumerate(FCONSTS):
        fo = o_consts + CO["fconst"] + i
        nc.const_aps.register(F32, v, AR[:, fo:fo + 1])

    def _finish():
        bld.final_drain("sync")
        with nc.Block() as block:
            @block.sync
            def _(e):
                for f in bld.streams["sync"]:
                    f(e)

            @block.tensor
            def _(e):
                for f in bld.streams["tensor"]:
                    f(e)

            @block.vector
            def _(e):
                for f in bld.streams["vector"]:
                    f(e)

            @block.scalar
            def _(e):
                for f in bld.streams["scalar"]:
                    f(e)

            @block.gpsimd
            def _(e):
                for f in bld.streams["gpsimd"]:
                    f(e)

        stack.close()
        return nc

    _tapn = [0]

    def tap(name, ap, rbufs):
        if name not in DBG_TAPS:
            return
        _tapn[0] += 1
        d = nc.dram_tensor(f"tap_{name}", list(ap.shape), F32,
                           kind="ExternalOutput").ap()
        em("scalar", lambda e, d=d, ap=ap: e.dma_start(out=d, in_=ap),
           reads=list(rbufs), writes=[Buf("tap")], dma=True)

    def mask_col(ci, c):
        mi = MASK_COL.get((ci, c))
        if mi is None:
            return None
        return CONSTS[:, CO["masks"] + mi:CO["masks"] + mi + 1]

    def emit_ln(xin, xout, rbufs, wbufs):
        m0 = AR[:, o_lnt:o_lnt + 1]
        m = AR[:, o_lnt + 1:o_lnt + 2]
        ssq = AR[:, o_lnt + 2:o_lnt + 3]
        srt = AR[:, o_lnt + 3:o_lnt + 4]
        rstd = AR[:, o_lnt + 4:o_lnt + 5]
        t0, t1, t2, t3 = Buf("ln0"), Buf("ln1"), Buf("ln2"), Buf("ln3")
        em("vector", lambda e: e.reduce_sum(m0, xin, axis=AX.X),
           reads=list(rbufs), writes=[t0])
        em("vector", lambda e: e.tensor_scalar_mul(m, m0, -1.0 / D),
           reads=[t0], writes=[t1])
        em("vector", lambda e: e.tensor_scalar(out=xout, in0=xin, scalar1=m,
                                               scalar2=None, op0=ALU.add),
           reads=list(rbufs) + [t1], writes=list(wbufs))
        em("vector", lambda e: e.scalar_tensor_tensor(
            out=LNS, in0=xout, scalar=1.0, in1=xout, op0=ALU.mult,
            op1=ALU.mult, accum_out=ssq),
           reads=list(wbufs), writes=[t2, Buf("lns")])
        em("scalar", lambda e: e.activation(srt, ssq, AF.Sqrt,
                                            bias=EPS, scale=1.0 / D),
           reads=[t2, b_consts], writes=[t3])
        em("vector", lambda e: e.reciprocal(rstd, srt),
           reads=[t3], writes=[t3])
        em("vector", lambda e: e.tensor_scalar_mul(xout, xout, rstd),
           reads=[t3], writes=list(wbufs))

    em("sync", lambda e: e.dma_start(out=CONSTS, in_=dt["consts"][:, :]),
       writes=[b_consts], dma=True)
    ar.mark()
    o_brow = ar.alloc(4096, "bias_rows")
    b_brow = Buf("brow")
    em("sync", lambda e: e.dma_start(
        out=AR[0:1, o_brow:o_brow + 3 * D],
        in_=dt["bqkv"].rearrange("(a d) -> a d", a=1)),
       writes=[b_brow], dma=True)
    em("sync", lambda e: e.dma_start(
        out=AR[0:1, o_brow + 3 * D:o_brow + 4 * D],
        in_=dt["bproj"].rearrange("(a d) -> a d", a=1)),
       writes=[b_brow], dma=True)
    em("sync", lambda e: e.dma_start(
        out=BQC, in_=dt["bqkv"][0:1024].rearrange("(c p) -> p c", p=128),
        allow_slow_non_contiguous=True),
       writes=[b_bqs], dma=True)
    em("sync", lambda e: e.dma_start(
        out=BKC, in_=dt["bqkv"][1024:2048].rearrange("(c p) -> p c", p=128),
        allow_slow_non_contiguous=True),
       writes=[b_bqs], dma=True)
    em("vector", lambda e: e.tensor_scalar_mul(BQC, BQC, SCALE),
       reads=[b_consts], writes=[b_bqs])
    for j in range(8):
        src = AR[0:1, o_brow + j * 512:o_brow + (j + 1) * 512]
        dst = (BQKVB[:, j * 512:(j + 1) * 512] if j < 6
               else BPROJB[:, (j - 6) * 512:(j - 5) * 512])
        pv = psv(0, j % 2)
        em("tensor", lambda e, s=src, p=pv: e.matmul(p, ONES_ROW, s,
                                                     start=True, stop=True),
           reads=[b_brow, b_consts], writes=[psb(0, j % 2)])
        em("vector", lambda e, p=pv, d=dst: e.tensor_copy(d, p),
           reads=[psb(0, j % 2)], writes=[b_bqkvb if j < 6 else b_bprojb])
    tap("bqkvb", BQKVB, [b_bqkvb])
    tap("bprojb", BPROJB, [b_bprojb])
    tap("bqc", AR[:, o_bqs:o_bqs + 16], [b_bqs])

    bld.barrier()
    ar.reset()
    ar.mark()
    o_wqk = ar.alloc(DC * 2048, "wqk")
    o_kf = ar.alloc(2 * 1024, "kf_tiles")
    o_xh = ar.alloc(1024, "p2_xhat")
    o_xt = ar.alloc(1024, "p2_xT")
    o_qk = ar.alloc(2048, "p2_qk")
    o_pr = ar.alloc(1024, "p2_prod")
    o_s16 = ar.alloc(32, "p2_s16")
    o_sacc = ar.alloc(16, "p2_sacc")
    o_sc = ar.alloc(768, "p2_score")
    b_wqk, b_kf0, b_kf1, b_xh, b_xt, b_qk, b_pr, b_sacc, b_s16 = (
        Buf(x) for x in ("wqk", "kf0", "kf1", "xh", "xt", "qk", "pr", "sacc",
                         "s16"))
    kf_bufs = [b_kf0, b_kf1]
    SACC = AR[:, o_sacc:o_sacc + 4]
    XH = AR[:, o_xh:o_xh + 1024]
    QKT = AR[:, o_qk:o_qk + 2048]
    PRD = AR[:, o_pr:o_pr + 1024]
    S16 = AR[:, o_s16:o_s16 + 16]
    S1 = AR[:, o_s16 + 16:o_s16 + 17]

    for c in range(DC):
        em("sync", lambda e, c=c: e.dma_start(
            out=AR[:, o_wqk + c * 2048:o_wqk + (c + 1) * 2048],
            in_=dt["wqkv"][c * 128:(c + 1) * 128, 0:2048]),
           writes=[b_wqk, b_brow], dma=True)
    em("vector", lambda e: e.memset(SACC, 0.0), writes=[b_sacc])

    def emit_transposes(src_ap, src_bufs, dst_off, dst_bufs, nb=8):
        for j in range(nb):
            ph = j % 2
            pv = psv(2, cols=(ph * 512, ph * 512 + 128))
            dst = AR[:, dst_off + j * 128:dst_off + (j + 1) * 128]
            db = dst_bufs[j % len(dst_bufs)]
            em("tensor", lambda e, j=j, p=pv, s=src_ap: e.transpose(
                p, s[:, j * 128:(j + 1) * 128], IDENT),
               reads=list(src_bufs) + [b_consts], writes=[psb(2, ph)])
            em("scalar", lambda e, p=pv, d=dst: e.copy(d, p),
               reads=[psb(2, ph)], writes=[db])

    for b in range(B):
        for kfi in range(4):
            ti = b * 4 + kfi
            par = ti % 2
            kf_ap = AR[:, o_kf + par * 1024:o_kf + (par + 1) * 1024]
            em("sync", lambda e, r=ti * 128, d=kf_ap: e.dma_start(
                out=d, in_=dt["kfx"][r:r + 128, :]),
               writes=[kf_bufs[par]], dma=True)
            emit_ln(kf_ap, XH, [kf_bufs[par]], [b_xh])
            emit_transposes(XH, [b_xh], o_xt, [b_xt])
            for g in range(4):
                pi, ph = g // 2, g % 2
                pv = psv(pi, ph)
                fns = [lambda e, c=c, g=g, pv=pv: e.matmul(
                    pv, AR[:, o_xt + c * 128:o_xt + (c + 1) * 128],
                    AR[:, o_wqk + c * 2048 + g * 512:o_wqk + c * 2048 + (g + 1) * 512],
                    start=(c == 0), stop=(c == DC - 1)) for c in range(DC)]
                em("tensor", fns, reads=[b_xt, b_wqk], writes=[psb(pi, ph)])
                em("vector", lambda e, g=g, pv=pv: e.tensor_add(
                    QKT[:, g * 512:(g + 1) * 512], pv,
                    BQKVB[:, g * 512:(g + 1) * 512]),
                   reads=[psb(pi, ph), b_bqkvb], writes=[b_qk])
            em("vector", lambda e: e.tensor_mul(PRD, QKT[:, 0:1024],
                                                QKT[:, 1024:2048]),
               reads=[b_qk], writes=[b_pr])
            em("vector", lambda e: e.reduce_sum(
                S16, PRD.rearrange("p (h d) -> p h d", h=H), axis=AX.X),
               reads=[b_pr], writes=[b_s16])
            em("vector", lambda e: e.reduce_max(S1, S16, axis=AX.X),
               reads=[b_s16], writes=[b_s16])
            em("vector", lambda e, kfi=kfi: e.tensor_add(
                SACC[:, kfi:kfi + 1], SACC[:, kfi:kfi + 1], S1),
               reads=[b_s16], writes=[b_sacc])
            if ti == 0:
                tap("xh0", XH, [b_xh])
                tap("xt0", AR[:, o_xt:o_xt + 1024], [b_xt])
                tap("qk0", QKT, [b_qk])
                tap("s16_0", S16, [b_sacc])

    SCORE = AR[0:4, o_sc:o_sc + 128]
    WRK = AR[0:4, o_sc + 128:o_sc + 256]
    MX8 = AR[0:4, o_sc + 256:o_sc + 264]
    PIDXU = AR[0:4, o_sc + 272:o_sc + 312]
    PIDXF = AR[0:4, o_sc + 320:o_sc + 360]
    G4 = AR[0:4, o_sc + 368:o_sc + 672]
    ZROW = AR[0:1, o_sc + 688:o_sc + 752]
    b_score, b_wrk, b_mx8, b_pidx, b_g4, b_zrow = (
        Buf(x) for x in ("score", "wrk", "mx8", "pidx", "g4", "zrow"))
    em("tensor", lambda e: e.transpose(PS[2][0:4, 0:128], SACC, IDENT),
       reads=[b_sacc, b_consts], writes=[psb(2, 0)])
    em("scalar", lambda e: e.mul(SCORE, PS[2][0:4, 0:128], 1.0 / B),
       reads=[psb(2, 0), b_consts], writes=[b_score])
    tap("sacc", SACC, [b_sacc])
    tap("score", SCORE, [b_score])
    em("vector", lambda e: e.tensor_copy(WRK, SCORE),
       reads=[b_score], writes=[b_wrk])
    for r in range(5):
        em("vector", lambda e: e.max(out=MX8, in_=WRK),
           reads=[b_wrk], writes=[b_mx8])
        em("vector", lambda e, r=r: e.max_index(
            out=PIDXU[:, r * 8:(r + 1) * 8].bitcast(U32),
            in_max=MX8, in_values=WRK),
           reads=[b_wrk, b_mx8], writes=[b_pidx])
        em("vector", lambda e: e.match_replace(out=WRK, in_to_replace=MX8,
                                               in_values=WRK, imm_value=-1e30),
           reads=[b_mx8], writes=[b_wrk])
    tap("wrk", WRK, [b_wrk])
    tap("pidxu", PIDXU, [b_pidx])
    em("vector", lambda e: e.tensor_copy(PIDXF, PIDXU.bitcast(U32)),
       reads=[b_pidx], writes=[b_pidx])
    tap("pidxf", PIDXF, [b_pidx])
    fns = [lambda e, fl=fl: e.tensor_scalar(
        out=G4[:, fl * TK:(fl + 1) * TK], in0=PIDXF[:, 0:TK],
        scalar1=BASENC, scalar2=float(128 * fl), op0=ALU.add, op1=ALU.add)
        for fl in range(FC)]
    em("vector", fns, reads=[b_pidx, b_consts], writes=[b_g4])
    b_gd = Buf("gdram")
    em("gpsimd", lambda e: e.dma_start(out=dt["gdram"][0:4 * GA],
                                       in_=G4[:, 0:GA]),
       reads=[b_g4], writes=[b_gd], dma=True)
    em("gpsimd", lambda e: e.dma_start(out=dt["gdram"][4 * GA:G],
                                       in_=G4[:, GA:GA + GB]),
       reads=[b_g4], writes=[b_gd], dma=True)
    em("vector", lambda e: e.memset(ZROW, 0.0), writes=[b_zrow])
    em("gpsimd", lambda e: e.dma_start(out=dt["gdram"][G:GP], in_=ZROW),
       reads=[b_zrow], writes=[b_gd], dma=True)
    em("gpsimd", lambda e: e.dma_start(
        out=GSBF, in_=dt["gdram"].rearrange("(c p) -> p c", p=128),
        allow_slow_non_contiguous=True),
       reads=[b_gd], writes=[b_small], dma=True)
    em("vector", lambda e: e.tensor_copy(GSBU.bitcast(U32), GSBF),
       reads=[b_small], writes=[b_small])
    ar.reset()

    if PHASES < 3:
        return _finish()
    bld.barrier()
    ar.mark()
    o_xgt = ar.alloc(DC * GP, "XgT")
    o_kt = ar.alloc(DC * GP, "KT")
    o_v65 = ar.alloc(GT * H * 65, "V65")
    ar.mark()
    o_xg = ar.alloc(2 * 1024, "Xg")
    o_wkv = ar.alloc(2 * 1024, "wkv")
    b_xgt, b_kt, b_v65 = Buf("XgT"), Buf("KT"), Buf("V65")
    xg_bufs = [Buf("xg0"), Buf("xg1")]
    wkv_bufs = [Buf("wkv0"), Buf("wkv1")]

    def xgt_chunk(c):
        return AR[:, o_xgt + c * GP:o_xgt + (c + 1) * GP]

    def kt_chunk(c):
        return AR[:, o_kt + c * GP:o_kt + (c + 1) * GP]

    def v65_tile(t):
        return AR[:, o_v65 + t * H * 65:o_v65 + (t + 1) * H * 65]

    for t in range(GT):
        par = t % 2
        xg = AR[:, o_xg + par * 1024:o_xg + (par + 1) * 1024]
        em("gpsimd", lambda e, t=t, xg=xg: e.indirect_dma_start(
            out=xg, out_offset=None, in_=dt["x"][:, :],
            in_offset=IndirectOffsetOnAxis(
                ap=GSBU[:, t:t + 1].bitcast(U32), axis=0)),
           reads=[b_small], writes=[xg_bufs[par]], dma=True)
        emit_ln(xg, xg, [xg_bufs[par]], [xg_bufs[par]])
        if t == 0:
            tap("xg0", xg, [xg_bufs[par]])
        for j in range(DC):
            ph = j % 2
            pv = psv(2, cols=(ph * 512, ph * 512 + 128))
            em("tensor", lambda e, j=j, xg=xg, p=pv: e.transpose(
                p, xg[:, j * 128:(j + 1) * 128], IDENT),
               reads=[xg_bufs[par], b_consts], writes=[psb(2, ph)])
            em("scalar", lambda e, j=j, t=t, p=pv: e.copy(
                xgt_chunk(j)[:, t * 128:(t + 1) * 128], p),
               reads=[psb(2, ph)], writes=[b_xgt])

    segs = [(0, 512), (512, 1024), (1024, 1280)]
    for si, (s0, s1) in enumerate(segs):
        for c in range(DC):
            par = c % 2
            wk = AR[:, o_wkv + par * 1024:o_wkv + (par + 1) * 1024]
            em("sync", lambda e, c=c, wk=wk: e.dma_start(
                out=wk, in_=dt["wqkv"][c * 128:(c + 1) * 128, 1024:2048]),
               writes=[wkv_bufs[par]], dma=True)
            fns = [lambda e, g=g, c=c, wk=wk, s0=s0, s1=s1: e.matmul(
                psv(g // 2, cols=((g % 2) * 512, (g % 2) * 512 + s1 - s0)),
                wk[:, g * 128:(g + 1) * 128], xgt_chunk(c)[:, s0:s1],
                start=(c == 0), stop=(c == DC - 1)) for g in range(8)]
            em("tensor", fns, reads=[wkv_bufs[par], b_xgt],
               writes=[psb(g // 2, g % 2) for g in range(8)])
        for g in range(8):
            pv = psv(g // 2, cols=((g % 2) * 512, (g % 2) * 512 + s1 - s0))
            em("scalar", lambda e, g=g, s0=s0, s1=s1, pv=pv: e.activation(
                kt_chunk(g)[:, s0:s1], pv, AF.Identity, bias=BKC[:, g:g + 1]),
               reads=[psb(g // 2, g % 2), b_bqs], writes=[b_kt])

    waves = [[(t, v) for t in range(4) for v in range(2)],
             [(t, v) for t in range(4, 8) for v in range(2)],
             [(t, v) for t in range(8, 10) for v in range(2)]]
    for wave in waves:
        for c in range(DC):
            par = c % 2
            wv = AR[:, o_wkv + par * 1024:o_wkv + (par + 1) * 1024]
            em("sync", lambda e, c=c, wv=wv: e.dma_start(
                out=wv, in_=dt["wqkv"][c * 128:(c + 1) * 128, 2048:3072]),
               writes=[wkv_bufs[par]], dma=True)
            fns = [lambda e, t=t, v=v, c=c, wv=wv, gi=gi: e.matmul(
                psv(gi // 2, gi % 2),
                xgt_chunk(c)[:, t * 128:(t + 1) * 128],
                wv[:, v * 512:(v + 1) * 512],
                start=(c == 0), stop=(c == DC - 1))
                for gi, (t, v) in enumerate(wave)]
            em("tensor", fns, reads=[wkv_bufs[par], b_xgt],
               writes=[psb(gi // 2, gi % 2) for gi in range(len(wave))])
        for gi, (t, v) in enumerate(wave):
            pv = psv(gi // 2, gi % 2)
            em("vector", lambda e, t=t, v=v, pv=pv: e.tensor_add(
                v65_tile(t)[:, v * 520:v * 520 + 520]
                .rearrange("p (h d) -> p h d", d=65)[:, :, 0:64],
                pv.rearrange("p (h d) -> p h d", d=64),
                BQKVB[:, 2048 + v * 512:2048 + (v + 1) * 512]
                .rearrange("p (h d) -> p h d", d=64)),
               reads=[psb(gi // 2, gi % 2), b_bqkvb], writes=[b_v65])
    for t in range(GT):
        em("vector", lambda e, t=t: e.memset(
            v65_tile(t).rearrange("p (h d) -> p h d", d=65)[:, :, 64:65], 1.0),
           writes=[b_v65])
    tap("kt0", kt_chunk(0), [b_kt])
    tap("v650", v65_tile(0), [b_v65])
    tap("xgt0", xgt_chunk(0), [b_xgt])

    if PHASES < 4:
        return _finish()
    bld.barrier()
    ar.reset()
    o_wq = o_xgt
    b_wq = Buf("Wq")
    o_wp = ar.alloc(DC * 1024, "Wproj")
    b_wp = Buf("Wp")
    o_x4 = ar.alloc(2 * 1024, "x_tiles")
    x4_bufs = [Buf("x4a"), Buf("x4b")]
    o_xh4 = ar.alloc(1024, "xhat4")
    xh4_cb = [Buf(f"xh4_{j}") for j in range(DC)]
    o_qt = ar.alloc(1024, "QT")
    b_qt = Buf("QT")
    o_pt = ar.alloc(GP, "PT")
    b_pt = Buf("PT")
    o_ot = ar.alloc(1024, "OT")
    b_ot = Buf("OT")
    o_osb = ar.alloc(80, "o_sb")
    b_osb, b_rc = Buf("osb"), Buf("rc")
    o_x1t = ar.alloc(1024, "x1t")
    b_x1t = Buf("x1t")
    b_x1s = Buf("x1s")

    QT = AR[:, o_qt:o_qt + 1024]
    PT = AR[:, o_pt:o_pt + GP]
    OT = AR[:, o_ot:o_ot + 1024]
    OSB = AR[:, o_osb:o_osb + 64]
    RC = AR[:, o_osb + 64:o_osb + 65]
    X1T = AR[:, o_x1t:o_x1t + 1024]
    XH4 = AR[:, o_xh4:o_xh4 + 1024]

    for c in range(DC):
        em("sync", lambda e, c=c: e.dma_start(
            out=AR[:, o_wq + c * 1024:o_wq + (c + 1) * 1024],
            in_=dt["wqkv"][c * 128:(c + 1) * 128, 0:1024]),
           reads=[b_xgt], writes=[b_wq], dma=True)
        em("sync", lambda e, c=c: e.dma_start(
            out=AR[:, o_wp + c * 1024:o_wp + (c + 1) * 1024],
            in_=dt["wproj"][c * 128:(c + 1) * 128, :]),
           writes=[b_wp], dma=True)

    for ci in range(KCL):
        chunks = CLUSTER_CHUNKS[ci]
        nch = len(chunks)
        for qi in range(FC):
            tile = ci * FC + qi
            par = tile % 2
            xa = AR[:, o_x4 + par * 1024:o_x4 + (par + 1) * 1024]
            em("sync", lambda e, t=tile, xa=xa: e.dma_start(
                out=xa, in_=dt["x"][t * 128:(t + 1) * 128, :]),
               writes=[x4_bufs[par]], dma=True)
            emit_ln(xa, XH4, [x4_bufs[par]], xh4_cb)
            for j in range(DC):
                ph = j % 2
                pv = psv(2, cols=(ph * 512, ph * 512 + 128))
                em("tensor", lambda e, j=j, p=pv: e.transpose(
                    p, XH4[:, j * 128:(j + 1) * 128], IDENT),
                   reads=[xh4_cb[j], b_consts], writes=[psb(2, ph)])
                em("scalar", lambda e, j=j, p=pv: e.copy(
                    XH4[:, j * 128:(j + 1) * 128], p),
                   reads=[psb(2, ph)], writes=[xh4_cb[j]])
            for qc in range(DC):
                ph = qc % 2
                pv = psv(3, cols=(ph * 512, ph * 512 + 128))
                fns = [lambda e, qc=qc, c=c, pv=pv: e.matmul(
                    pv, AR[:, o_wq + c * 1024 + qc * 128:
                           o_wq + c * 1024 + (qc + 1) * 128],
                    XH4[:, c * 128:(c + 1) * 128],
                    start=(c == 0), stop=(c == DC - 1)) for c in range(DC)]
                em("tensor", fns, reads=[b_wq] + xh4_cb, writes=[psb(3, ph)])
                em("scalar", lambda e, qc=qc, pv=pv: e.activation(
                    QT[:, qc * 128:(qc + 1) * 128], pv, AF.Identity,
                    bias=BQC[:, qc:qc + 1], scale=SCALE),
                   reads=[psb(3, ph), b_bqs, b_consts], writes=[b_qt])
            if tile == 0:
                tap("qt0", QT, [b_qt])
            for h in range(H):
                stp = h % 2
                qrhs = QT[(h % 2) * 64:(h % 2) * 64 + 64,
                          (h // 2) * 128:(h // 2) * 128 + 128]
                groups = [chunks[i:i + 8] for i in range(0, nch, 8)]
                for gi, grp in enumerate(groups):
                    fns = []
                    for j, (c, _) in enumerate(grp):
                        pv = psv(stp, cols=(j * 128, (j + 1) * 128))
                        kl = kt_chunk(h // 2)[(h % 2) * 64:(h % 2) * 64 + 64,
                                              c * 128:(c + 1) * 128]
                        fns.append(lambda e, pv=pv, kl=kl, q=qrhs: e.matmul(
                            pv, kl, q, start=True, stop=True))
                    em("tensor", fns, reads=[b_kt, b_qt],
                       writes=[psb(stp, 0), psb(stp, 1)])
                    ncols = len(grp) * 128
                    em("scalar", lambda e, gi=gi, nc_=ncols, stp=stp: e.activation(
                        PT[:, gi * 1024:gi * 1024 + nc_],
                        psv(stp, cols=(0, nc_)), AF.Exp),
                       reads=[psb(stp, 0), psb(stp, 1)], writes=[b_pt])
                fns = []
                for j, (c, _) in enumerate(chunks):
                    mc = mask_col(ci, c)
                    if mc is not None:
                        fns.append(lambda e, j=j, mc=mc: e.tensor_scalar_mul(
                            PT[:, j * 128:(j + 1) * 128],
                            PT[:, j * 128:(j + 1) * 128], mc))
                if fns:
                    em("vector", fns, reads=[b_consts], writes=[b_pt])
                opv = psv(2, cols=(0, 65))
                fns = [lambda e, j=j, c=c, h=h, opv=opv, last=(j == nch - 1):
                       e.matmul(
                    opv, PT[:, j * 128:(j + 1) * 128],
                    v65_tile(c)[:, h * 65:(h + 1) * 65],
                    start=(j == 0), stop=last)
                    for j, (c, _) in enumerate(chunks)]
                em("tensor", fns, reads=[b_pt, b_v65], writes=[psb(2, 0)])
                em("vector", lambda e: e.reciprocal(RC, psv(2, cols=(64, 65))),
                   reads=[psb(2, 0)], writes=[b_rc])
                em("vector", lambda e: e.tensor_scalar_mul(
                    OSB, psv(2, cols=(0, 64)), RC),
                   reads=[psb(2, 0), b_rc], writes=[b_osb])
                potv = PS[2][0:64, 512:640]
                em("tensor", lambda e, potv=potv: e.transpose(potv, OSB, IDENT),
                   reads=[b_osb, b_consts], writes=[psb(2, 1)])
                em("scalar", lambda e, h=h, potv=potv: e.copy(
                    OT[(h % 2) * 64:(h % 2) * 64 + 64,
                       (h // 2) * 128:(h // 2) * 128 + 128], potv),
                   reads=[psb(2, 1)], writes=[b_ot])
                if tile == 0 and h == 0:
                    tap("pt0", PT, [b_pt])
                    tap("osb0", OSB, [b_osb])
            for dh in range(2):
                pv = psv(dh, cols=(0, 512))
                fns = [lambda e, c=c, dh=dh, pv=pv: e.matmul(
                    pv, OT[:, c * 128:(c + 1) * 128],
                    AR[:, o_wp + c * 1024 + dh * 512:
                       o_wp + c * 1024 + (dh + 1) * 512],
                    start=(c == 0), stop=(c == DC - 1)) for c in range(DC)]
                em("tensor", fns, reads=[b_ot, b_wp], writes=[psb(dh, 0)])
                em("vector", lambda e, dh=dh, pv=pv: e.tensor_add(
                    X1T[:, dh * 512:(dh + 1) * 512], pv,
                    BPROJB[:, dh * 512:(dh + 1) * 512]),
                   reads=[psb(dh, 0), b_bprojb], writes=[b_x1t])
                em("vector", lambda e, dh=dh, xa=xa: e.tensor_add(
                    X1T[:, dh * 512:(dh + 1) * 512],
                    X1T[:, dh * 512:(dh + 1) * 512],
                    xa[:, dh * 512:(dh + 1) * 512]),
                   reads=[x4_bufs[par]], writes=[b_x1t])
            if tile == 0:
                tap("ot0", OT, [b_ot])
                tap("x1t0", X1T, [b_x1t])
            em("scalar", lambda e, t=tile: e.dma_start(
                out=dt["x1s"][t * 128:(t + 1) * 128, :], in_=X1T),
               reads=[b_x1t], writes=[b_x1s], dma=True)

    if PHASES < 5:
        return _finish()
    bld.barrier()
    ar.reset()
    o_tid = ar.alloc(3 * NT, "tid_tables")
    TIDG = AR[:, o_tid:o_tid + NT]
    TIDS = AR[:, o_tid + NT:o_tid + 2 * NT]
    GSLOT = AR[:, o_tid + 2 * NT:o_tid + 3 * NT]
    b_tid = Buf("tid")
    ar.mark()
    o_s8 = ar.alloc(5 * 4096, "slot8")
    MASKT = AR[0:8, o_s8:o_s8 + 4096]
    CUMT = AR[0:8, o_s8 + 4096:o_s8 + 8192]
    ZERO8 = AR[0:8, o_s8 + 8192:o_s8 + 12288]
    KEEPT = AR[0:8, o_s8 + 12288:o_s8 + 16384]
    PKT = AR[0:8, o_s8 + 16384:o_s8 + 20480]
    b_s8 = Buf("s8")
    o_r = ar.alloc(2 * 1024, "r_x1")
    r_bufs = [Buf("rx0"), Buf("rx1")]
    o_rxh = ar.alloc(1024, "r_xh")
    b_rxh = Buf("rxh")
    o_rxt = ar.alloc(1024, "r_xT")
    b_rxt = Buf("rxt")
    o_wg = ar.alloc(DC * 8, "wg")
    b_wg = Buf("wg")
    o_lg = ar.alloc(64, "logits")
    b_lg = Buf("lg")
    o_eg = ar.alloc(2 * NT, "eix_gate")
    b_eg = Buf("eg")
    o_ps5 = ar.alloc(16, "pair_slot")
    b_pair = Buf("pair")
    o_pairs = ar.alloc(4 * NT, "pairs_tmp")
    b_pairs = Buf("pairs")

    LG = AR[:, o_lg:o_lg + 8]
    NM = AR[:, o_lg + 8:o_lg + 9]
    RS = AR[:, o_lg + 9:o_lg + 10]
    RMAXP = AR[:, o_lg + 10:o_lg + 11]
    MSK = AR[:, o_lg + 16:o_lg + 24]
    EXPS = AR[:, o_lg + 24:o_lg + 32]
    KTOK = AR[:, o_lg + 32:o_lg + 33]
    PTOK = AR[:, o_lg + 33:o_lg + 34]
    EF = AR[:, o_lg + 34:o_lg + 35]
    BASE = AR[:, o_lg + 35:o_lg + 36]
    SLOTF = AR[:, o_lg + 36:o_lg + 37]
    SLOTU = AR[:, o_lg + 37:o_lg + 38]
    PM = AR[:, o_lg + 38:o_lg + 39]
    PD = AR[:, o_lg + 39:o_lg + 40]
    R1 = AR[:, o_lg + 40:o_lg + 41]
    MX8R = AR[:, o_lg + 48:o_lg + 56]
    IX8 = AR[:, o_lg + 56:o_lg + 64]
    EIX = AR[:, o_eg:o_eg + NT]
    GATE = AR[:, o_eg + NT:o_eg + 2 * NT]
    PAIR = AR[:, o_ps5:o_ps5 + 2]
    RXH = AR[:, o_rxh:o_rxh + 1024]
    PAIRS = AR[:, o_pairs:o_pairs + 2 * NT]
    TMPM = AR[:, o_pairs + 2 * NT:o_pairs + 3 * NT]
    TMP2 = AR[:, o_pairs + 3 * NT:o_pairs + 4 * NT]

    em("sync", lambda e: e.dma_start(
        out=AR[:, o_wg:o_wg + DC * 8].rearrange("p (c ee) -> p c ee", ee=8),
        in_=dt["wg"].rearrange("(c p) ee -> p c ee", p=128)),
       reads=[b_x1s], writes=[b_wg], dma=True)
    em("vector", lambda e: e.memset(ZERO8, 0.0), writes=[b_s8])
    b_slots_d = Buf("slots_dram")
    em("gpsimd", lambda e: e.dma_start(
        out=dt["slots"][0:N, :].rearrange("(p j) v -> p (j v)", p=128),
        in_=PREFILL),
       reads=[b_consts], writes=[b_slots_d], dma=True)

    for t in range(NT):
        par = t % 2
        xr = AR[:, o_r + par * 1024:o_r + (par + 1) * 1024]
        em("sync", lambda e, t=t, xr=xr: e.dma_start(
            out=xr, in_=dt["x1s"][t * 128:(t + 1) * 128, :]),
           reads=[b_x1s], writes=[r_bufs[par]], dma=True)
        emit_ln(xr, RXH, [r_bufs[par]], [b_rxh])
        emit_transposes(RXH, [b_rxh], o_rxt, [b_rxt])
        pv = psv(2, cols=(0, 8))
        fns = [lambda e, c=c, pv=pv: e.matmul(
            pv, AR[:, o_rxt + c * 128:o_rxt + (c + 1) * 128],
            AR[:, o_wg + c * 8:o_wg + (c + 1) * 8],
            start=(c == 0), stop=(c == DC - 1)) for c in range(DC)]
        em("tensor", fns, reads=[b_rxt, b_wg], writes=[psb(2, 0)])
        b_nm, b_rs, b_msk, b_ix8, b_mx8r = (
            Buf("nm"), Buf("rs"), Buf("msk"), Buf("ix8"), Buf("mx8r"))
        em("vector", lambda e, pv=pv: e.tensor_copy(LG, pv),
           reads=[psb(2, 0)], writes=[b_lg])
        em("vector", lambda e: e.max(out=MX8R, in_=LG),
           reads=[b_lg], writes=[b_mx8r])
        em("vector", lambda e: e.tensor_scalar_mul(NM, MX8R[:, 0:1], -1.0),
           reads=[b_mx8r], writes=[b_nm])
        em("scalar", lambda e: e.activation(EXPS, LG, AF.Exp,
                                            bias=NM, accum_out=RS),
           reads=[b_lg, b_nm], writes=[b_rs])
        em("vector", lambda e, t=t: e.reciprocal(GATE[:, t:t + 1], RS),
           reads=[b_rs], writes=[b_eg])
        em("vector", lambda e: e.max_index(out=IX8.bitcast(U32),
                                           in_max=MX8R, in_values=LG),
           reads=[b_lg, b_mx8r], writes=[b_ix8])
        em("vector", lambda e, t=t: e.tensor_copy(
            EIX[:, t:t + 1].bitcast(U32), IX8[:, 0:1].bitcast(U32)),
           reads=[b_ix8], writes=[b_eg])
        em("vector", lambda e: e.tensor_scalar(out=MSK, in0=LG, scalar1=NM,
                                               scalar2=0.0, op0=ALU.add,
                                               op1=ALU.is_equal),
           reads=[b_lg, b_nm], writes=[b_msk])
        em("tensor", lambda e: e.transpose(PS[3][0:8, 0:128], MSK, IDENT),
           reads=[b_msk, b_consts], writes=[psb(3, 0)])
        em("vector", lambda e, t=t: e.tensor_copy(
            MASKT[:, t * 128:(t + 1) * 128], PS[3][0:8, 0:128]),
           reads=[psb(3, 0)], writes=[b_s8])

    em("vector", lambda e: e.tensor_tensor_scan(
        out=CUMT, data0=MASKT, data1=ZERO8, initial=0.0,
        op0=ALU.add, op1=ALU.add), reads=[b_s8], writes=[b_s8])
    em("vector", lambda e: e.tensor_sub(CUMT, CUMT, MASKT),
       reads=[b_s8], writes=[b_s8])
    em("vector", lambda e: e.tensor_scalar(out=KEEPT, in0=CUMT,
                                           scalar1=float(CAP), scalar2=None,
                                           op0=ALU.is_lt),
       reads=[b_s8], writes=[b_s8])
    em("vector", lambda e: e.tensor_mul(KEEPT, KEEPT, MASKT),
       reads=[b_s8], writes=[b_s8])
    em("vector", lambda e: e.tensor_mul(PKT, CUMT, KEEPT),
       reads=[b_s8], writes=[b_s8])

    tap("maskt", MASKT, [b_s8])
    tap("cumt", CUMT, [b_s8])
    tap("keept", KEEPT, [b_s8])
    for t in range(NT):
        em("tensor", [
            lambda e, t=t: e.transpose(
                PS[3][0:128, 0:8], KEEPT[:, t * 128:(t + 1) * 128],
                IDENT[0:8, 0:8]),
            lambda e, t=t: e.transpose(
                PS[3][0:128, 16:24], PKT[:, t * 128:(t + 1) * 128],
                IDENT[0:8, 0:8]),
        ], reads=[b_s8, b_consts], writes=[psb(3, 0)])
        b_sl = Buf("sl")
        seq = [
            lambda e: e.reduce_sum(KTOK, PS[3][0:128, 0:8], axis=AX.X),
            lambda e: e.reduce_sum(PTOK, PS[3][0:128, 16:24], axis=AX.X),
            lambda e, t=t: e.tensor_copy(EF, EIX[:, t:t + 1].bitcast(U32)),
            lambda e: e.tensor_single_scalar(PD, PTOK, 128.0, ALU.is_ge),
            lambda e: e.tensor_single_scalar(PM, PTOK, 256.0, ALU.is_ge),
            lambda e: e.tensor_add(PD, PD, PM),
            lambda e: e.tensor_single_scalar(PM, PTOK, 384.0, ALU.is_ge),
            lambda e: e.tensor_add(PD, PD, PM),
            lambda e: e.tensor_scalar_mul(PM, PD, -4095.0),
            lambda e: e.scalar_tensor_tensor(out=R1, in0=PTOK, scalar=32.0,
                                             in1=PM, op0=ALU.mult, op1=ALU.add),
            lambda e: e.tensor_scalar(out=BASE, in0=EF, scalar1=4.0,
                                      scalar2=R1, op0=ALU.mult, op1=ALU.add),
            lambda e, t=t: e.tensor_sub(BASE, BASE, ALT[:, t:t + 1]),
            lambda e, t=t: e.scalar_tensor_tensor(
                out=SLOTF, in0=BASE, scalar=KTOK, in1=ALT[:, t:t + 1],
                op0=ALU.mult, op1=ALU.add),
            lambda e: e.tensor_copy(SLOTU.bitcast(U32), SLOTF),
            lambda e, t=t: e.tensor_copy(PAIR[:, 0:1], TOKID[:, t:t + 1]),
            lambda e, t=t: e.tensor_copy(PAIR[:, 1:2], GATE[:, t:t + 1]),
        ]
        for fi, f in enumerate(seq):
            em("vector", f, reads=[psb(3, 0), b_eg, b_consts, b_sl],
               writes=[b_pair if fi >= len(seq) - 3 else b_sl])
        em("gpsimd", lambda e: e.indirect_dma_start(
            out=dt["slots"][:, :],
            out_offset=IndirectOffsetOnAxis(ap=SLOTU.bitcast(U32), axis=0),
            in_=PAIR, in_offset=None),
           reads=[b_pair, b_slots_d], writes=[b_slots_d], dma=True)

    em("gpsimd", lambda e: e.dma_start(
        out=PAIRS,
        in_=dt["slots"][0:N, :].rearrange("(p j) v -> p (j v)", p=128)),
       reads=[b_slots_d], writes=[b_pairs], dma=True)
    PAIRS3 = PAIRS.rearrange("p (j v) -> p j v", v=2)
    b_rb = Buf("rb")
    for f in [
        lambda e: e.tensor_scalar_max(TMPM, PAIRS3[:, :, 0:1], 0.0),
        lambda e: e.tensor_copy(TIDG.bitcast(U32), TMPM),
        lambda e: e.tensor_copy(GSLOT, PAIRS3[:, :, 1:2]),
        lambda e: e.tensor_scalar(out=TMPM, in0=PAIRS3[:, :, 0:1], scalar1=0.0,
                                  scalar2=None, op0=ALU.is_ge),
        lambda e: e.tensor_sub(TMP2, PAIRS3[:, :, 0:1], ALTR),
        lambda e: e.tensor_mul(TMP2, TMP2, TMPM),
        lambda e: e.tensor_add(TMP2, TMP2, ALTR),
        lambda e: e.tensor_copy(TIDS.bitcast(U32), TMP2),
    ]:
        em("vector", f, reads=[b_pairs, b_consts, b_rb],
           writes=[b_tid, b_rb])

    tap("tidg", TIDG, [b_tid])
    tap("tids", TIDS, [b_tid])
    tap("gslot", GSLOT, [b_tid])
    tap("eix", EIX, [b_eg])
    tap("gate", GATE, [b_eg])

    if PHASES < 7:
        return _finish()
    bld.barrier()
    ar.reset()
    o_ht = ar.alloc(32 * 512, "hT")
    b_ht = Buf("hT")
    o_xe = ar.alloc(4 * 1024, "Xe")
    xe_bufs = [Buf(f"xe{j}") for j in range(4)]
    o_xet = ar.alloc(DC * 512, "XeT")
    b_xet = Buf("XeT")
    o_w1 = ar.alloc(2 * 1024, "w1buf")
    w1_bufs = [Buf("w1a"), Buf("w1b")]
    o_w2 = ar.alloc(2 * 1024, "w2buf")
    w2_bufs = [Buf("w2a"), Buf("w2b")]
    o_oe = ar.alloc(4 * 1024, "Oe")
    b_oe = Buf("Oe")

    def ht_chunk(i):
        return AR[:, o_ht + i * 512:o_ht + (i + 1) * 512]

    for ex in range(min(E, NEXPERTS)):
        for j in range(4):
            xe = AR[:, o_xe + j * 1024:o_xe + (j + 1) * 1024]
            em("gpsimd", lambda e, ex=ex, j=j, xe=xe: e.indirect_dma_start(
                out=xe, out_offset=None, in_=dt["x1s"][0:N, :],
                in_offset=IndirectOffsetOnAxis(
                    ap=TIDG[:, ex * 4 + j:ex * 4 + j + 1].bitcast(U32),
                    axis=0)),
               reads=[b_tid, b_x1s], writes=[xe_bufs[j]], dma=True)
            emit_ln(xe, xe, [xe_bufs[j]], [xe_bufs[j]])
            if P7STEPS < 2:
                continue
            for c in range(DC):
                ph = c % 2
                pv = psv(2, cols=(ph * 512, ph * 512 + 128))
                em("tensor", lambda e, c=c, xe=xe, pv=pv: e.transpose(
                    pv, xe[:, c * 128:(c + 1) * 128], IDENT),
                   reads=[xe_bufs[j], b_consts], writes=[psb(2, ph)])
                em("scalar", lambda e, c=c, j=j, pv=pv: e.copy(
                    AR[:, o_xet + c * 512 + j * 128:
                       o_xet + c * 512 + (j + 1) * 128], pv),
                   reads=[psb(2, ph)], writes=[b_xet])
        if P7STEPS < 3:
            continue
        for hfc in range(32):
            par = hfc % 2
            w1 = AR[:, o_w1 + par * 1024:o_w1 + (par + 1) * 1024]
            em("sync", lambda e, ex=ex, hfc=hfc, w1=w1: e.dma_start(
                out=w1, in_=dt["w1c"][ex, hfc]),
               writes=[w1_bufs[par]], dma=True)
            pv = psv(3, par)
            if FFN1_MODE != 1:
                fns = [lambda e, c=c, w1=w1, pv=pv: e.matmul(
                    pv, w1[:, c * 128:(c + 1) * 128],
                    AR[:, o_xet + c * 512:o_xet + (c + 1) * 512],
                    start=(c == 0), stop=(c == DC - 1)) for c in range(DC)]
                em("tensor", fns, reads=[w1_bufs[par], b_xet],
                   writes=[psb(3, par)])
            if FFN1_MODE == 2:
                em("vector", lambda e, hfc=hfc, pv=pv: e.tensor_copy(
                    ht_chunk(hfc), pv),
                   reads=[psb(3, par)], writes=[b_ht])
            elif FFN1_MODE == 0:
                em("scalar", lambda e, hfc=hfc, pv=pv: e.activation(
                    ht_chunk(hfc), pv, AF.Gelu_apprx_tanh),
                   reads=[psb(3, par)], writes=[b_ht])
        if P7STEPS < 4:
            continue
        for hfc in range(32):
            par = hfc % 2
            w2 = AR[:, o_w2 + par * 1024:o_w2 + (par + 1) * 1024]
            em("sync", lambda e, ex=ex, hfc=hfc, w2=w2: e.dma_start(
                out=w2, in_=dt["w2"][ex, hfc * 128:(hfc + 1) * 128, :]),
               writes=[w2_bufs[par]], dma=True)
            fns = [lambda e, g=g, hfc=hfc, w2=w2: e.matmul(
                psv(g // 2, g % 2),
                ht_chunk(hfc)[:, (g // 2) * 128:(g // 2 + 1) * 128],
                w2[:, (g % 2) * 512:(g % 2 + 1) * 512],
                start=(hfc == 0), stop=(hfc == 31)) for g in range(8)]
            em("tensor", fns, reads=[w2_bufs[par], b_ht],
               writes=[psb(g // 2, g % 2) for g in range(8)])
        for g in range(8):
            t, dh = g // 2, g % 2
            em("vector", lambda e, ex=ex, t=t, dh=dh, g=g: e.tensor_scalar_mul(
                AR[:, o_oe + t * 1024 + dh * 512:
                   o_oe + t * 1024 + (dh + 1) * 512],
                psv(g // 2, g % 2), GSLOT[:, ex * 4 + t:ex * 4 + t + 1]),
               reads=[psb(g // 2, g % 2), b_tid], writes=[b_oe])
        if ex == 0:
            tap("xet0", AR[:, o_xet:o_xet + 512], [b_xet])
            tap("ht0", ht_chunk(0), [b_ht])
            tap("oe0", AR[:, o_oe:o_oe + 1024], [b_oe])
        if P7STEPS < 5:
            continue
        for j in range(4):
            em("gpsimd", lambda e, ex=ex, j=j: e.indirect_dma_start(
                out=dt["x1s"][:, :],
                out_offset=IndirectOffsetOnAxis(
                    ap=TIDS[:, ex * 4 + j:ex * 4 + j + 1].bitcast(U32),
                    axis=0),
                in_=AR[:, o_oe + j * 1024:o_oe + (j + 1) * 1024],
                in_offset=None, compute_op=ALU.add),
               reads=[b_oe, b_tid], writes=[b_x1s], dma=True)

    if PHASES < 8:
        return _finish()
    b_out = Buf("out")
    for t in range(NT):
        em("sync", lambda e, t=t: e.dma_start(
            out=dt["out"][t * 128:(t + 1) * 128, :],
            in_=dt["x1s"][t * 128:(t + 1) * 128, :]),
           reads=[b_x1s], writes=[b_out], dma=True)
    return _finish()


_KF_ROWS = np.concatenate([np.arange(f * 128, (f + 1) * 128)
                           for f in (0, 8, 16, 24)])


def kernel(L0_attn_x, L0_attn_W_qkv, L0_attn_b_qkv, L0_attn_W_proj,
           L0_attn_b_proj, L0_attn_clusters, L0_attn_keyframes,
           L0_ffn_Wg, L0_ffn_W1, L0_ffn_W2):
    x = np.ascontiguousarray(np.asarray(L0_attn_x, np.float32))
    clusters = np.asarray(L0_attn_clusters)
    keyframes = np.asarray(L0_attn_keyframes)
    assert np.array_equal(
        clusters, np.arange(32, dtype=clusters.dtype).reshape(4, 8))
    assert np.array_equal(keyframes, clusters[:, 0])

    kfx = np.ascontiguousarray(x[:, _KF_ROWS, :].reshape(B * 512, D))
    w1 = np.ascontiguousarray(np.asarray(L0_ffn_W1, np.float32))
    w1c = np.ascontiguousarray(
        w1.reshape(E, DC, 128, 32, 128).transpose(0, 3, 2, 1, 4)
        .reshape(E, 32, 128, DC * 128))

    nc = build_program()
    shared = {
        "kfx": kfx,
        "wqkv": np.ascontiguousarray(np.asarray(L0_attn_W_qkv, np.float32)),
        "bqkv": np.ascontiguousarray(np.asarray(L0_attn_b_qkv, np.float32)),
        "wproj": np.ascontiguousarray(np.asarray(L0_attn_W_proj, np.float32)),
        "bproj": np.ascontiguousarray(np.asarray(L0_attn_b_proj, np.float32)),
        "wg": np.ascontiguousarray(np.asarray(L0_ffn_Wg, np.float32)),
        "w1c": w1c,
        "w2": np.ascontiguousarray(np.asarray(L0_ffn_W2, np.float32)),
        "consts": _make_consts(),
    }
    in_maps = [dict(shared, x=np.ascontiguousarray(x[b])) for b in range(B)]
    res = run_bass_kernel_spmd(nc, in_maps, core_ids=list(range(B)))
    return np.stack([res.results[b]["out"] for b in range(B)], axis=0)


# version 02
# speedup vs baseline: 1.0776x; 1.0776x over previous
import contextlib
import numpy as np
import concourse.bass as bass
import concourse.mybir as mybir
from concourse.bass_utils import run_bass_kernel_spmd
from concourse.bass_core_rust import IndirectOffsetOnAxis

F32 = mybir.dt.float32
U32 = mybir.dt.uint32
I32 = mybir.dt.int32
AF = mybir.ActivationFunctionType
ALU = mybir.AluOpType
AX = mybir.AxisListType

B, N, D, H, DH = 8, 4096, 1024, 16, 64
NT, DC = N // 128, D // 128
KCL, FC = 4, 8
TK, NSUB = 38, 5
GA, GB = NSUB * TK, (FC - NSUB) * TK
G = KCL * (GA + GB)
GP, GT = 1280, 10
CAP, E, HF = 512, 8, 4096
SCALE = DH ** -0.5
EPS = 1e-5
ARENA_COLS = 52992


def _cluster_chunks():
    out = []
    for ci in range(KCL):
        need = np.zeros(GP, bool)
        need[0:4 * GA + GB] = True
        if ci > 0:
            s = 4 * GA + GB * ci
            need[s:s + GB] = True
        chunks = []
        for c in range(GT):
            seg = need[c * 128:(c + 1) * 128]
            if seg.any():
                chunks.append((c, None if seg.all() else seg.astype(np.float32)))
        out.append(chunks)
    return out


CLUSTER_CHUNKS = _cluster_chunks()
N_MASKS = sum(1 for ch in CLUSTER_CHUNKS for (_, m) in ch if m is not None)


FCONSTS = [EPS, 1.0 / D, SCALE]


def _const_layout():
    off, o = {}, 0
    for name, n in (("ident", 128), ("ones_row", 128), ("masks", max(N_MASKS, 1)),
                    ("alt", NT), ("tokid", NT), ("altr", NT), ("basenc", 1),
                    ("prefill", 64), ("fconst", len(FCONSTS))):
        off[name] = o
        o += n
    return off, o


CONST_OFF, CONST_COLS = _const_layout()


def _make_consts():
    c = np.zeros((128, CONST_COLS), np.float32)
    o = CONST_OFF
    c[:, o["ident"]:o["ident"] + 128] = np.eye(128, dtype=np.float32)
    c[0, o["ones_row"]:o["ones_row"] + 128] = 1.0
    mi = 0
    for ch in CLUSTER_CHUNKS:
        for (_, m) in ch:
            if m is not None:
                c[:, o["masks"] + mi] = m
                mi += 1
    tok = (np.arange(128)[:, None] + 128 * np.arange(NT)[None, :]).astype(np.float32)
    c[:, o["tokid"]:o["tokid"] + NT] = tok
    c[:, o["alt"]:o["alt"] + NT] = tok + 4096.0
    rowid = (32 * np.arange(128)[:, None] + np.arange(NT)[None, :]).astype(np.float32)
    c[:, o["altr"]:o["altr"] + NT] = rowid + 4096.0
    c[0:4, o["basenc"]] = np.arange(4, dtype=np.float32) * 1024.0
    pf = np.zeros((128, 64), np.float32)
    pf[:, 0::2] = -1.0
    c[:, o["prefill"]:o["prefill"] + 64] = pf
    for i, v in enumerate(FCONSTS):
        c[:, o["fconst"] + i] = np.float32(v)
    return c


MASK_COL = {}
_mi = 0
for _ci, _ch in enumerate(CLUSTER_CHUNKS):
    for (_c, _m) in _ch:
        if _m is not None:
            MASK_COL[(_ci, _c)] = _mi
            _mi += 1


class Sem:
    def __init__(self, stack, nc, name):
        self.h = stack.enter_context(nc.semaphore(name))
        self.n = 0

    def inc(self, k=1):
        self.n += k
        return self.n


class Buf:
    __slots__ = ("name", "ready", "readers", "dsem_hw", "dsem_sw")

    def __init__(self, name=""):
        self.name = name
        self.ready = None
        self.readers = {}
        self.dsem_hw = None
        self.dsem_sw = None


class Arena:
    def __init__(self):
        self.off = 0
        self.marks = []

    def alloc(self, cols, name=""):
        o = (self.off + 15) // 16 * 16
        assert o + cols <= ARENA_COLS, f"arena overflow at {name}: {o}+{cols}"
        self.off = o + cols
        return o

    def mark(self):
        self.marks.append(self.off)

    def reset(self):
        self.off = self.marks.pop()


class Builder:

    ENGS = ("sync", "tensor", "vector", "scalar", "gpsimd")

    def __init__(self, nc, stack):
        self.nc = nc
        self.stack = stack
        self.streams = {e: [] for e in self.ENGS}
        self.comp = {e: Sem(stack, nc, f"c_{e}") for e in self.ENGS}
        self.dsems = []

    def _dsem(self, buf, sw):
        attr = "dsem_sw" if sw else "dsem_hw"
        s = getattr(buf, attr)
        if s is None:
            s = Sem(self.stack, self.nc, f"ds{len(self.dsems)}")
            setattr(buf, attr, s)
            self.dsems.append(s)
        return s

    def emit(self, eng, fns, reads=(), writes=(), dma=False):
        if callable(fns):
            fns = [fns]
        if dma:
            assert len(writes) >= 1
            sem = self._dsem(writes[0], sw=(eng == "gpsimd"))
        else:
            sem = self.comp[eng]
        waits = {}

        def add(sv):
            if sv is None:
                return
            s, v = sv
            if waits.get(s, 0) < v:
                waits[s] = v

        for b in reads:
            add(b.ready)
        for b in writes:
            if not (dma and b.ready is not None and b.ready[0] is sem
                    and not b.readers):
                add(b.ready)
            for s, v in b.readers.items():
                add((s, v))
        k = 16 if dma else 1
        tgt = sem.inc(k)
        for b in writes:
            b.ready = (sem, tgt)
            b.readers = {}
        for b in reads:
            b.readers[sem] = tgt
        wl = list(waits.items())

        def run(e, wl=wl, fns=fns, sem=sem, k=k):
            for s, v in wl:
                e.wait_ge(s.h, v)
            ins = None
            for f in fns:
                ins = f(e)
            ins.then_inc(sem.h, k)

        self.streams[eng].append(run)

    def barrier(self):
        targets = ([(s, s.n) for s in self.comp.values() if s.n > 0]
                   + [(s, s.n) for s in self.dsems if s.n > 0])
        for e in self.ENGS:
            own = self.comp[e]
            wl = [(s, v) for s, v in targets if s is not own]

            def run(eng, wl=wl):
                for s, v in wl:
                    eng.wait_ge(s.h, v)

            self.streams[e].append(run)

    def final_drain(self, eng):
        sems = [(s, s.n) for s in self.dsems]

        def run(e, sems=sems):
            for s, v in sems:
                e.wait_ge(s.h, v)

        self.streams[eng].append(run)


DBG_TAPS = set()
PHASES = 99
NEXPERTS = E
P7STEPS = 5
FFN1_MODE = 0


def build_program(dbg=False):
    nc = bass.Bass("TRN2", target_bir_lowering=False)
    stack = contextlib.ExitStack()
    scratch_kind = "ExternalOutput" if dbg else "Internal"

    dt = {}
    dt["x"] = nc.dram_tensor("x", [N, D], F32, kind="ExternalInput").ap()
    dt["kfx"] = nc.dram_tensor("kfx", [B * 512, D], F32, kind="ExternalInput").ap()
    dt["wqkv"] = nc.dram_tensor("wqkv", [D, 3 * D], F32, kind="ExternalInput").ap()
    dt["bqkv"] = nc.dram_tensor("bqkv", [3 * D], F32, kind="ExternalInput").ap()
    dt["wproj"] = nc.dram_tensor("wproj", [D, D], F32, kind="ExternalInput").ap()
    dt["bproj"] = nc.dram_tensor("bproj", [D], F32, kind="ExternalInput").ap()
    dt["wg"] = nc.dram_tensor("wg", [D, E], F32, kind="ExternalInput").ap()
    dt["w1c"] = nc.dram_tensor("w1c", [E, 32, 128, DC * 128], F32,
                               kind="ExternalInput").ap()
    dt["w2"] = nc.dram_tensor("w2", [E, HF, D], F32, kind="ExternalInput").ap()
    dt["consts"] = nc.dram_tensor("consts", [128, CONST_COLS], F32,
                                  kind="ExternalInput").ap()
    dt["out"] = nc.dram_tensor("out", [N, D], F32, kind="ExternalOutput").ap()
    dt["x1s"] = nc.dram_tensor("x1s", [2 * N, D], F32, kind=scratch_kind).ap()
    dt["gdram"] = nc.dram_tensor("gdram", [GP], F32, kind=scratch_kind).ap()
    dt["slots"] = nc.dram_tensor("slots", [2 * N, 2], F32, kind=scratch_kind).ap()

    arena_t = stack.enter_context(nc.sbuf_tensor("arena", [128, ARENA_COLS], F32))
    AR = arena_t[:, :]
    psum_t = [stack.enter_context(nc.psum_tensor(f"ps{i}", [128, 1024], F32))
              for i in range(4)]
    PS = [p[:, :] for p in psum_t]
    ps_buf = [Buf(f"ps{i}{h}") for i in range(4) for h in (0, 1)]

    def psv(i, half=None, cols=None):
        if cols is not None:
            return PS[i][:, cols[0]:cols[1]]
        return PS[i][:, half * 512:(half + 1) * 512]

    def psb(i, half):
        return ps_buf[2 * i + half]

    bld = Builder(nc, stack)
    ar = Arena()
    em = bld.emit
    TS, VS, SS = bld.comp["tensor"], bld.comp["vector"], bld.comp["scalar"]
    CO = CONST_OFF

    o_consts = ar.alloc(CONST_COLS, "consts")
    o_bqkvb = ar.alloc(3 * D, "bqkv_bcast")
    o_bprojb = ar.alloc(D, "bproj_bcast")
    o_bqs = ar.alloc(16, "bq_bk_cols")
    o_small = ar.alloc(128, "smalls")
    o_lnt = ar.alloc(16, "ln_tmp")
    o_lns = ar.alloc(1024, "ln_scratch")
    b_consts, b_bqkvb, b_bprojb, b_bqs, b_small = (
        Buf(x) for x in ("consts", "bqkvb", "bprojb", "bqs", "small"))

    CONSTS = AR[:, o_consts:o_consts + CONST_COLS]
    IDENT = CONSTS[:, CO["ident"]:CO["ident"] + 128]
    ONES_ROW = AR[0:1, o_consts + CO["ones_row"]:o_consts + CO["ones_row"] + 128]
    TOKID = CONSTS[:, CO["tokid"]:CO["tokid"] + NT]
    ALT = CONSTS[:, CO["alt"]:CO["alt"] + NT]
    ALTR = CONSTS[:, CO["altr"]:CO["altr"] + NT]
    BASENC = AR[0:4, o_consts + CO["basenc"]:o_consts + CO["basenc"] + 1]
    PREFILL = CONSTS[:, CO["prefill"]:CO["prefill"] + 64]
    BQKVB = AR[:, o_bqkvb:o_bqkvb + 3 * D]
    BPROJB = AR[:, o_bprojb:o_bprojb + D]
    BQC = AR[:, o_bqs:o_bqs + 8]
    BKC = AR[:, o_bqs + 8:o_bqs + 16]
    LNS = AR[:, o_lns:o_lns + 1024]
    GSBF = AR[:, o_small:o_small + GT]
    GSBU = AR[:, o_small + 16:o_small + 16 + GT]

    for i, v in enumerate(FCONSTS):
        fo = o_consts + CO["fconst"] + i
        nc.const_aps.register(F32, v, AR[:, fo:fo + 1])

    def _finish():
        bld.final_drain("sync")
        with nc.Block() as block:
            @block.sync
            def _(e):
                for f in bld.streams["sync"]:
                    f(e)

            @block.tensor
            def _(e):
                for f in bld.streams["tensor"]:
                    f(e)

            @block.vector
            def _(e):
                for f in bld.streams["vector"]:
                    f(e)

            @block.scalar
            def _(e):
                for f in bld.streams["scalar"]:
                    f(e)

            @block.gpsimd
            def _(e):
                for f in bld.streams["gpsimd"]:
                    f(e)

        stack.close()
        return nc

    _tapn = [0]

    def tap(name, ap, rbufs):
        if name not in DBG_TAPS:
            return
        _tapn[0] += 1
        d = nc.dram_tensor(f"tap_{name}", list(ap.shape), F32,
                           kind="ExternalOutput").ap()
        em("scalar", lambda e, d=d, ap=ap: e.dma_start(out=d, in_=ap),
           reads=list(rbufs), writes=[Buf("tap")], dma=True)

    def mask_col(ci, c):
        mi = MASK_COL.get((ci, c))
        if mi is None:
            return None
        return CONSTS[:, CO["masks"] + mi:CO["masks"] + mi + 1]

    def emit_ln(xin, xout, rbufs, wbufs):
        m0 = AR[:, o_lnt:o_lnt + 1]
        m = AR[:, o_lnt + 1:o_lnt + 2]
        ssq = AR[:, o_lnt + 2:o_lnt + 3]
        srt = AR[:, o_lnt + 3:o_lnt + 4]
        rstd = AR[:, o_lnt + 4:o_lnt + 5]
        t0, t1, t2, t3 = Buf("ln0"), Buf("ln1"), Buf("ln2"), Buf("ln3")
        em("vector", lambda e: e.reduce_sum(m0, xin, axis=AX.X),
           reads=list(rbufs), writes=[t0])
        em("vector", lambda e: e.tensor_scalar_mul(m, m0, -1.0 / D),
           reads=[t0], writes=[t1])
        em("vector", lambda e: e.tensor_scalar(out=xout, in0=xin, scalar1=m,
                                               scalar2=None, op0=ALU.add),
           reads=list(rbufs) + [t1], writes=list(wbufs))
        em("vector", lambda e: e.scalar_tensor_tensor(
            out=LNS, in0=xout, scalar=1.0, in1=xout, op0=ALU.mult,
            op1=ALU.mult, accum_out=ssq),
           reads=list(wbufs), writes=[t2, Buf("lns")])
        em("scalar", lambda e: e.activation(srt, ssq, AF.Sqrt,
                                            bias=EPS, scale=1.0 / D),
           reads=[t2, b_consts], writes=[t3])
        em("vector", lambda e: e.reciprocal(rstd, srt),
           reads=[t3], writes=[t3])
        em("vector", lambda e: e.tensor_scalar_mul(xout, xout, rstd),
           reads=[t3], writes=list(wbufs))

    em("sync", lambda e: e.dma_start(out=CONSTS, in_=dt["consts"][:, :]),
       writes=[b_consts], dma=True)
    ar.mark()
    o_brow = ar.alloc(4096, "bias_rows")
    b_brow = Buf("brow")
    em("sync", lambda e: e.dma_start(
        out=AR[0:1, o_brow:o_brow + 3 * D],
        in_=dt["bqkv"].rearrange("(a d) -> a d", a=1)),
       writes=[b_brow], dma=True)
    em("sync", lambda e: e.dma_start(
        out=AR[0:1, o_brow + 3 * D:o_brow + 4 * D],
        in_=dt["bproj"].rearrange("(a d) -> a d", a=1)),
       writes=[b_brow], dma=True)
    em("sync", lambda e: e.dma_start(
        out=BQC, in_=dt["bqkv"][0:1024].rearrange("(c p) -> p c", p=128),
        allow_slow_non_contiguous=True),
       writes=[b_bqs], dma=True)
    em("sync", lambda e: e.dma_start(
        out=BKC, in_=dt["bqkv"][1024:2048].rearrange("(c p) -> p c", p=128),
        allow_slow_non_contiguous=True),
       writes=[b_bqs], dma=True)
    em("vector", lambda e: e.tensor_scalar_mul(BQC, BQC, SCALE),
       reads=[b_consts], writes=[b_bqs])
    for j in range(8):
        src = AR[0:1, o_brow + j * 512:o_brow + (j + 1) * 512]
        dst = (BQKVB[:, j * 512:(j + 1) * 512] if j < 6
               else BPROJB[:, (j - 6) * 512:(j - 5) * 512])
        pv = psv(0, j % 2)
        em("tensor", lambda e, s=src, p=pv: e.matmul(p, ONES_ROW, s,
                                                     start=True, stop=True),
           reads=[b_brow, b_consts], writes=[psb(0, j % 2)])
        em("vector", lambda e, p=pv, d=dst: e.tensor_copy(d, p),
           reads=[psb(0, j % 2)], writes=[b_bqkvb if j < 6 else b_bprojb])
    tap("bqkvb", BQKVB, [b_bqkvb])
    tap("bprojb", BPROJB, [b_bprojb])
    tap("bqc", AR[:, o_bqs:o_bqs + 16], [b_bqs])

    bld.barrier()
    ar.reset()
    ar.mark()
    o_wqk = ar.alloc(DC * 2048, "wqk")
    o_kf = ar.alloc(2 * 1024, "kf_tiles")
    o_xh = ar.alloc(1024, "p2_xhat")
    o_xt = ar.alloc(1024, "p2_xT")
    o_qk = ar.alloc(2048, "p2_qk")
    o_pr = ar.alloc(1024, "p2_prod")
    o_s16 = ar.alloc(32, "p2_s16")
    o_sacc = ar.alloc(16, "p2_sacc")
    o_sc = ar.alloc(768, "p2_score")
    b_wqk, b_kf0, b_kf1, b_xh, b_xt, b_qk, b_pr, b_sacc, b_s16 = (
        Buf(x) for x in ("wqk", "kf0", "kf1", "xh", "xt", "qk", "pr", "sacc",
                         "s16"))
    kf_bufs = [b_kf0, b_kf1]
    SACC = AR[:, o_sacc:o_sacc + 4]
    XH = AR[:, o_xh:o_xh + 1024]
    QKT = AR[:, o_qk:o_qk + 2048]
    PRD = AR[:, o_pr:o_pr + 1024]
    S16 = AR[:, o_s16:o_s16 + 16]
    S1 = AR[:, o_s16 + 16:o_s16 + 17]

    for c in range(DC):
        em("sync", lambda e, c=c: e.dma_start(
            out=AR[:, o_wqk + c * 2048:o_wqk + (c + 1) * 2048],
            in_=dt["wqkv"][c * 128:(c + 1) * 128, 0:2048]),
           writes=[b_wqk, b_brow], dma=True)
    em("vector", lambda e: e.memset(SACC, 0.0), writes=[b_sacc])

    def emit_transposes(src_ap, src_bufs, dst_off, dst_bufs, nb=8):
        for j in range(nb):
            ph = j % 2
            pv = psv(2, cols=(ph * 512, ph * 512 + 128))
            dst = AR[:, dst_off + j * 128:dst_off + (j + 1) * 128]
            db = dst_bufs[j % len(dst_bufs)]
            em("tensor", lambda e, j=j, p=pv, s=src_ap: e.transpose(
                p, s[:, j * 128:(j + 1) * 128], IDENT),
               reads=list(src_bufs) + [b_consts], writes=[psb(2, ph)])
            em("scalar", lambda e, p=pv, d=dst: e.copy(d, p),
               reads=[psb(2, ph)], writes=[db])

    for b in range(B):
        for kfi in range(4):
            ti = b * 4 + kfi
            par = ti % 2
            kf_ap = AR[:, o_kf + par * 1024:o_kf + (par + 1) * 1024]
            em("sync", lambda e, r=ti * 128, d=kf_ap: e.dma_start(
                out=d, in_=dt["kfx"][r:r + 128, :]),
               writes=[kf_bufs[par]], dma=True)
            emit_ln(kf_ap, XH, [kf_bufs[par]], [b_xh])
            emit_transposes(XH, [b_xh], o_xt, [b_xt])
            for g in range(4):
                pi, ph = g // 2, g % 2
                pv = psv(pi, ph)
                fns = [lambda e, c=c, g=g, pv=pv: e.matmul(
                    pv, AR[:, o_xt + c * 128:o_xt + (c + 1) * 128],
                    AR[:, o_wqk + c * 2048 + g * 512:o_wqk + c * 2048 + (g + 1) * 512],
                    start=(c == 0), stop=(c == DC - 1)) for c in range(DC)]
                em("tensor", fns, reads=[b_xt, b_wqk], writes=[psb(pi, ph)])
                em("vector", lambda e, g=g, pv=pv: e.tensor_add(
                    QKT[:, g * 512:(g + 1) * 512], pv,
                    BQKVB[:, g * 512:(g + 1) * 512]),
                   reads=[psb(pi, ph), b_bqkvb], writes=[b_qk])
            em("vector", lambda e: e.tensor_mul(PRD, QKT[:, 0:1024],
                                                QKT[:, 1024:2048]),
               reads=[b_qk], writes=[b_pr])
            em("vector", lambda e: e.reduce_sum(
                S16, PRD.rearrange("p (h d) -> p h d", h=H), axis=AX.X),
               reads=[b_pr], writes=[b_s16])
            em("vector", lambda e: e.reduce_max(S1, S16, axis=AX.X),
               reads=[b_s16], writes=[b_s16])
            em("vector", lambda e, kfi=kfi: e.tensor_add(
                SACC[:, kfi:kfi + 1], SACC[:, kfi:kfi + 1], S1),
               reads=[b_s16], writes=[b_sacc])
            if ti == 0:
                tap("xh0", XH, [b_xh])
                tap("xt0", AR[:, o_xt:o_xt + 1024], [b_xt])
                tap("qk0", QKT, [b_qk])
                tap("s16_0", S16, [b_sacc])

    SCORE = AR[0:4, o_sc:o_sc + 128]
    WRK = AR[0:4, o_sc + 128:o_sc + 256]
    MX8 = AR[0:4, o_sc + 256:o_sc + 264]
    PIDXU = AR[0:4, o_sc + 272:o_sc + 312]
    PIDXF = AR[0:4, o_sc + 320:o_sc + 360]
    G4 = AR[0:4, o_sc + 368:o_sc + 672]
    ZROW = AR[0:1, o_sc + 688:o_sc + 752]
    b_score, b_wrk, b_mx8, b_pidx, b_g4, b_zrow = (
        Buf(x) for x in ("score", "wrk", "mx8", "pidx", "g4", "zrow"))
    em("tensor", lambda e: e.transpose(PS[2][0:4, 0:128], SACC, IDENT),
       reads=[b_sacc, b_consts], writes=[psb(2, 0)])
    em("scalar", lambda e: e.mul(SCORE, PS[2][0:4, 0:128], 1.0 / B),
       reads=[psb(2, 0), b_consts], writes=[b_score])
    tap("sacc", SACC, [b_sacc])
    tap("score", SCORE, [b_score])
    em("vector", lambda e: e.tensor_copy(WRK, SCORE),
       reads=[b_score], writes=[b_wrk])
    for r in range(5):
        em("vector", lambda e: e.max(out=MX8, in_=WRK),
           reads=[b_wrk], writes=[b_mx8])
        em("vector", lambda e, r=r: e.max_index(
            out=PIDXU[:, r * 8:(r + 1) * 8].bitcast(U32),
            in_max=MX8, in_values=WRK),
           reads=[b_wrk, b_mx8], writes=[b_pidx])
        em("vector", lambda e: e.match_replace(out=WRK, in_to_replace=MX8,
                                               in_values=WRK, imm_value=-1e30),
           reads=[b_mx8], writes=[b_wrk])
    tap("wrk", WRK, [b_wrk])
    tap("pidxu", PIDXU, [b_pidx])
    em("vector", lambda e: e.tensor_copy(PIDXF, PIDXU.bitcast(U32)),
       reads=[b_pidx], writes=[b_pidx])
    tap("pidxf", PIDXF, [b_pidx])
    fns = [lambda e, fl=fl: e.tensor_scalar(
        out=G4[:, fl * TK:(fl + 1) * TK], in0=PIDXF[:, 0:TK],
        scalar1=BASENC, scalar2=float(128 * fl), op0=ALU.add, op1=ALU.add)
        for fl in range(FC)]
    em("vector", fns, reads=[b_pidx, b_consts], writes=[b_g4])
    b_gd = Buf("gdram")
    em("gpsimd", lambda e: e.dma_start(out=dt["gdram"][0:4 * GA],
                                       in_=G4[:, 0:GA]),
       reads=[b_g4], writes=[b_gd], dma=True)
    em("gpsimd", lambda e: e.dma_start(out=dt["gdram"][4 * GA:G],
                                       in_=G4[:, GA:GA + GB]),
       reads=[b_g4], writes=[b_gd], dma=True)
    em("vector", lambda e: e.memset(ZROW, 0.0), writes=[b_zrow])
    em("gpsimd", lambda e: e.dma_start(out=dt["gdram"][G:GP], in_=ZROW),
       reads=[b_zrow], writes=[b_gd], dma=True)
    em("gpsimd", lambda e: e.dma_start(
        out=GSBF, in_=dt["gdram"].rearrange("(c p) -> p c", p=128),
        allow_slow_non_contiguous=True),
       reads=[b_gd], writes=[b_small], dma=True)
    em("vector", lambda e: e.tensor_copy(GSBU.bitcast(U32), GSBF),
       reads=[b_small], writes=[b_small])
    ar.reset()

    if PHASES < 3:
        return _finish()
    bld.barrier()
    ar.mark()
    o_xgt = ar.alloc(DC * GP, "XgT")
    o_kt = ar.alloc(DC * GP, "KT")
    o_v65 = ar.alloc(GT * H * 65, "V65")
    ar.mark()
    o_xg = ar.alloc(2 * 1024, "Xg")
    o_wkv = ar.alloc(2 * 1024, "wkv")
    b_xgt, b_kt, b_v65 = Buf("XgT"), Buf("KT"), Buf("V65")
    xg_bufs = [Buf("xg0"), Buf("xg1")]
    wkv_bufs = [Buf("wkv0"), Buf("wkv1")]

    def xgt_chunk(c):
        return AR[:, o_xgt + c * GP:o_xgt + (c + 1) * GP]

    def kt_chunk(c):
        return AR[:, o_kt + c * GP:o_kt + (c + 1) * GP]

    def v65_tile(t):
        return AR[:, o_v65 + t * H * 65:o_v65 + (t + 1) * H * 65]

    for t in range(GT):
        par = t % 2
        xg = AR[:, o_xg + par * 1024:o_xg + (par + 1) * 1024]
        em("gpsimd", lambda e, t=t, xg=xg: e.indirect_dma_start(
            out=xg, out_offset=None, in_=dt["x"][:, :],
            in_offset=IndirectOffsetOnAxis(
                ap=GSBU[:, t:t + 1].bitcast(U32), axis=0)),
           reads=[b_small], writes=[xg_bufs[par]], dma=True)
        emit_ln(xg, xg, [xg_bufs[par]], [xg_bufs[par]])
        if t == 0:
            tap("xg0", xg, [xg_bufs[par]])
        for j in range(DC):
            ph = j % 2
            pv = psv(2, cols=(ph * 512, ph * 512 + 128))
            em("tensor", lambda e, j=j, xg=xg, p=pv: e.transpose(
                p, xg[:, j * 128:(j + 1) * 128], IDENT),
               reads=[xg_bufs[par], b_consts], writes=[psb(2, ph)])
            em("scalar", lambda e, j=j, t=t, p=pv: e.copy(
                xgt_chunk(j)[:, t * 128:(t + 1) * 128], p),
               reads=[psb(2, ph)], writes=[b_xgt])

    segs = [(0, 512), (512, 1024), (1024, 1280)]
    for si, (s0, s1) in enumerate(segs):
        for c in range(DC):
            par = c % 2
            wk = AR[:, o_wkv + par * 1024:o_wkv + (par + 1) * 1024]
            em("sync", lambda e, c=c, wk=wk: e.dma_start(
                out=wk, in_=dt["wqkv"][c * 128:(c + 1) * 128, 1024:2048]),
               writes=[wkv_bufs[par]], dma=True)
            fns = [lambda e, g=g, c=c, wk=wk, s0=s0, s1=s1: e.matmul(
                psv(g // 2, cols=((g % 2) * 512, (g % 2) * 512 + s1 - s0)),
                wk[:, g * 128:(g + 1) * 128], xgt_chunk(c)[:, s0:s1],
                start=(c == 0), stop=(c == DC - 1)) for g in range(8)]
            em("tensor", fns, reads=[wkv_bufs[par], b_xgt],
               writes=[psb(g // 2, g % 2) for g in range(8)])
        for g in range(8):
            pv = psv(g // 2, cols=((g % 2) * 512, (g % 2) * 512 + s1 - s0))
            em("scalar", lambda e, g=g, s0=s0, s1=s1, pv=pv: e.activation(
                kt_chunk(g)[:, s0:s1], pv, AF.Identity, bias=BKC[:, g:g + 1]),
               reads=[psb(g // 2, g % 2), b_bqs], writes=[b_kt])

    waves = [[(t, v) for t in range(4) for v in range(2)],
             [(t, v) for t in range(4, 8) for v in range(2)],
             [(t, v) for t in range(8, 10) for v in range(2)]]
    for wave in waves:
        for c in range(DC):
            par = c % 2
            wv = AR[:, o_wkv + par * 1024:o_wkv + (par + 1) * 1024]
            em("sync", lambda e, c=c, wv=wv: e.dma_start(
                out=wv, in_=dt["wqkv"][c * 128:(c + 1) * 128, 2048:3072]),
               writes=[wkv_bufs[par]], dma=True)
            fns = [lambda e, t=t, v=v, c=c, wv=wv, gi=gi: e.matmul(
                psv(gi // 2, gi % 2),
                xgt_chunk(c)[:, t * 128:(t + 1) * 128],
                wv[:, v * 512:(v + 1) * 512],
                start=(c == 0), stop=(c == DC - 1))
                for gi, (t, v) in enumerate(wave)]
            em("tensor", fns, reads=[wkv_bufs[par], b_xgt],
               writes=[psb(gi // 2, gi % 2) for gi in range(len(wave))])
        for gi, (t, v) in enumerate(wave):
            pv = psv(gi // 2, gi % 2)
            em("vector", lambda e, t=t, v=v, pv=pv: e.tensor_add(
                v65_tile(t)[:, v * 520:v * 520 + 520]
                .rearrange("p (h d) -> p h d", d=65)[:, :, 0:64],
                pv.rearrange("p (h d) -> p h d", d=64),
                BQKVB[:, 2048 + v * 512:2048 + (v + 1) * 512]
                .rearrange("p (h d) -> p h d", d=64)),
               reads=[psb(gi // 2, gi % 2), b_bqkvb], writes=[b_v65])
    for t in range(GT):
        em("vector", lambda e, t=t: e.memset(
            v65_tile(t).rearrange("p (h d) -> p h d", d=65)[:, :, 64:65], 1.0),
           writes=[b_v65])
    tap("kt0", kt_chunk(0), [b_kt])
    tap("v650", v65_tile(0), [b_v65])
    tap("xgt0", xgt_chunk(0), [b_xgt])

    if PHASES < 4:
        return _finish()
    bld.barrier()
    ar.reset()
    o_wq = o_xgt
    b_wq = Buf("Wq")
    o_wp = ar.alloc(DC * 1024, "Wproj")
    b_wp = Buf("Wp")
    o_x4 = ar.alloc(2 * 1024, "x_tiles")
    x4_bufs = [Buf("x4a"), Buf("x4b")]
    o_xh4 = ar.alloc(1024, "xhat4")
    xh4_cb = [Buf(f"xh4_{j}") for j in range(DC)]
    o_qt = o_xgt + DC * 1024
    qt_bufs = [Buf("QT0"), Buf("QT1")]
    o_pt = ar.alloc(2 * GP, "PT")
    pt_bufs = [Buf("PT0"), Buf("PT1")]
    o_ot = ar.alloc(1024, "OT")
    b_ot = Buf("OT")
    o_osb = ar.alloc(160, "o_sb")
    osb_bufs = [Buf("osb0"), Buf("osb1")]
    b_rc = Buf("rc")
    b_x1s = Buf("x1s")

    OT = AR[:, o_ot:o_ot + 1024]
    XH4 = AR[:, o_xh4:o_xh4 + 1024]
    X1T = XH4

    for c in range(DC):
        em("sync", lambda e, c=c: e.dma_start(
            out=AR[:, o_wq + c * 1024:o_wq + (c + 1) * 1024],
            in_=dt["wqkv"][c * 128:(c + 1) * 128, 0:1024]),
           reads=[b_xgt], writes=[b_wq], dma=True)
        em("sync", lambda e, c=c: e.dma_start(
            out=AR[:, o_wp + c * 1024:o_wp + (c + 1) * 1024],
            in_=dt["wproj"][c * 128:(c + 1) * 128, :]),
           writes=[b_wp], dma=True)

    for ci in range(KCL):
        chunks = CLUSTER_CHUNKS[ci]
        nch = len(chunks)
        for qi in range(FC):
            tile = ci * FC + qi
            par = tile % 2
            xa = AR[:, o_x4 + par * 1024:o_x4 + (par + 1) * 1024]
            QT = AR[:, o_qt + par * 1024:o_qt + (par + 1) * 1024]
            b_qt = qt_bufs[par]
            em("sync", lambda e, t=tile, xa=xa: e.dma_start(
                out=xa, in_=dt["x"][t * 128:(t + 1) * 128, :]),
               writes=[x4_bufs[par]], dma=True)
            emit_ln(xa, XH4, [x4_bufs[par]], xh4_cb)
            for j in range(DC):
                pv = psv(3, cols=(0, 128))
                em("tensor", lambda e, j=j, p=pv: e.transpose(
                    p, XH4[:, j * 128:(j + 1) * 128], IDENT),
                   reads=[xh4_cb[j], b_consts], writes=[psb(3, 0)])
                em("scalar", lambda e, j=j, p=pv: e.copy(
                    XH4[:, j * 128:(j + 1) * 128], p),
                   reads=[psb(3, 0)], writes=[xh4_cb[j]])
            for qc in range(DC):
                pv = psv(3, cols=(0, 128))
                fns = [lambda e, qc=qc, c=c, pv=pv: e.matmul(
                    pv, AR[:, o_wq + c * 1024 + qc * 128:
                           o_wq + c * 1024 + (qc + 1) * 128],
                    XH4[:, c * 128:(c + 1) * 128],
                    start=(c == 0), stop=(c == DC - 1)) for c in range(DC)]
                em("tensor", fns, reads=[b_wq] + xh4_cb, writes=[psb(3, 0)])
                em("scalar", lambda e, qc=qc, pv=pv, QT=QT: e.activation(
                    QT[:, qc * 128:(qc + 1) * 128], pv, AF.Identity,
                    bias=BQC[:, qc:qc + 1], scale=SCALE),
                   reads=[psb(3, 0), b_bqs, b_consts], writes=[b_qt])
            if tile == 0:
                tap("qt0", QT, [b_qt])

            def emit_st(h, QT=QT, b_qt=b_qt, chunks=chunks, nch=nch, ci=ci):
                hp = h % 2
                PT = AR[:, o_pt + hp * GP:o_pt + (hp + 1) * GP]
                qrhs = QT[(h % 2) * 64:(h % 2) * 64 + 64,
                          (h // 2) * 128:(h // 2) * 128 + 128]
                groups = [chunks[i:i + 8] for i in range(0, nch, 8)]
                for gi, grp in enumerate(groups):
                    fns = []
                    for j, (c, _) in enumerate(grp):
                        pv = psv(hp, cols=(j * 128, (j + 1) * 128))
                        kl = kt_chunk(h // 2)[(h % 2) * 64:(h % 2) * 64 + 64,
                                              c * 128:(c + 1) * 128]
                        fns.append(lambda e, pv=pv, kl=kl, q=qrhs: e.matmul(
                            pv, kl, q, start=True, stop=True))
                    em("tensor", fns, reads=[b_kt, b_qt],
                       writes=[psb(hp, 0), psb(hp, 1)])
                    ncols = len(grp) * 128
                    em("scalar", lambda e, gi=gi, nc_=ncols, hp=hp, PT=PT:
                       e.activation(PT[:, gi * 1024:gi * 1024 + nc_],
                                    psv(hp, cols=(0, nc_)), AF.Exp),
                       reads=[psb(hp, 0), psb(hp, 1)], writes=[pt_bufs[hp]])
                fns = []
                for j, (c, _) in enumerate(chunks):
                    mc = mask_col(ci, c)
                    if mc is not None:
                        fns.append(lambda e, j=j, mc=mc, PT=PT:
                                   e.tensor_scalar_mul(
                            PT[:, j * 128:(j + 1) * 128],
                            PT[:, j * 128:(j + 1) * 128], mc))
                if fns:
                    em("vector", fns, reads=[b_consts], writes=[pt_bufs[hp]])

            def emit_pv(h, chunks=chunks, nch=nch, tile=tile):
                hp = h % 2
                PT = AR[:, o_pt + hp * GP:o_pt + (hp + 1) * GP]
                opv = psv(2, cols=(hp * 512, hp * 512 + 65))
                fns = [lambda e, j=j, c=c, h=h, opv=opv, PT=PT,
                       last=(j == nch - 1): e.matmul(
                    opv, PT[:, j * 128:(j + 1) * 128],
                    v65_tile(c)[:, h * 65:(h + 1) * 65],
                    start=(j == 0), stop=last)
                    for j, (c, _) in enumerate(chunks)]
                em("tensor", fns, reads=[pt_bufs[hp], b_v65],
                   writes=[psb(2, hp)])
                RC = AR[:, o_osb + 144 + hp:o_osb + 145 + hp]
                OSB = AR[:, o_osb + hp * 64:o_osb + (hp + 1) * 64]
                em("vector", lambda e, hp=hp, RC=RC: e.reciprocal(
                    RC, psv(2, cols=(hp * 512 + 64, hp * 512 + 65))),
                   reads=[psb(2, hp)], writes=[b_rc])
                em("vector", lambda e, hp=hp, RC=RC, OSB=OSB:
                   e.tensor_scalar_mul(OSB, psv(2, cols=(hp * 512,
                                                         hp * 512 + 64)), RC),
                   reads=[psb(2, hp), b_rc], writes=[osb_bufs[hp]])
                if tile == 0 and h == 0:
                    tap("pt0", PT, [pt_bufs[hp]])
                    tap("osb0", OSB, [osb_bufs[hp]])

            def emit_ot(h):
                hp = h % 2
                OSB = AR[:, o_osb + hp * 64:o_osb + (hp + 1) * 64]
                potv = PS[3][0:64, 512:640]
                em("tensor", lambda e, potv=potv, OSB=OSB: e.transpose(
                    potv, OSB, IDENT),
                   reads=[osb_bufs[hp], b_consts], writes=[psb(3, 1)])
                em("scalar", lambda e, h=h, potv=potv: e.copy(
                    OT[(h % 2) * 64:(h % 2) * 64 + 64,
                       (h // 2) * 128:(h // 2) * 128 + 128], potv),
                   reads=[psb(3, 1)], writes=[b_ot])

            for h in range(H):
                emit_st(h)
                if h >= 2:
                    emit_ot(h - 2)
                if h >= 1:
                    emit_pv(h - 1)
            emit_ot(H - 2)
            emit_pv(H - 1)
            emit_ot(H - 1)

            for dh in range(2):
                pv = psv(dh, cols=(0, 512))
                fns = [lambda e, c=c, dh=dh, pv=pv: e.matmul(
                    pv, OT[:, c * 128:(c + 1) * 128],
                    AR[:, o_wp + c * 1024 + dh * 512:
                       o_wp + c * 1024 + (dh + 1) * 512],
                    start=(c == 0), stop=(c == DC - 1)) for c in range(DC)]
                em("tensor", fns, reads=[b_ot, b_wp], writes=[psb(dh, 0)])
                em("vector", lambda e, dh=dh, pv=pv: e.tensor_add(
                    X1T[:, dh * 512:(dh + 1) * 512], pv,
                    BPROJB[:, dh * 512:(dh + 1) * 512]),
                   reads=[psb(dh, 0), b_bprojb],
                   writes=xh4_cb[dh * 4:(dh + 1) * 4])
                em("vector", lambda e, dh=dh, xa=xa: e.tensor_add(
                    X1T[:, dh * 512:(dh + 1) * 512],
                    X1T[:, dh * 512:(dh + 1) * 512],
                    xa[:, dh * 512:(dh + 1) * 512]),
                   reads=[x4_bufs[par]], writes=xh4_cb[dh * 4:(dh + 1) * 4])
            if tile == 0:
                tap("ot0", OT, [b_ot])
                tap("x1t0", X1T, xh4_cb)
            em("scalar", lambda e, t=tile: e.dma_start(
                out=dt["x1s"][t * 128:(t + 1) * 128, :], in_=X1T),
               reads=list(xh4_cb), writes=[b_x1s], dma=True)

    if PHASES < 5:
        return _finish()
    bld.barrier()
    ar.reset()
    o_tid = ar.alloc(3 * NT, "tid_tables")
    TIDG = AR[:, o_tid:o_tid + NT]
    TIDS = AR[:, o_tid + NT:o_tid + 2 * NT]
    GSLOT = AR[:, o_tid + 2 * NT:o_tid + 3 * NT]
    b_tid = Buf("tid")
    ar.mark()
    o_s8 = ar.alloc(5 * 4096, "slot8")
    MASKT = AR[0:8, o_s8:o_s8 + 4096]
    CUMT = AR[0:8, o_s8 + 4096:o_s8 + 8192]
    ZERO8 = AR[0:8, o_s8 + 8192:o_s8 + 12288]
    KEEPT = AR[0:8, o_s8 + 12288:o_s8 + 16384]
    PKT = AR[0:8, o_s8 + 16384:o_s8 + 20480]
    b_s8 = Buf("s8")
    o_r = ar.alloc(2 * 1024, "r_x1")
    r_bufs = [Buf("rx0"), Buf("rx1")]
    o_rxh = ar.alloc(1024, "r_xh")
    b_rxh = Buf("rxh")
    o_rxt = ar.alloc(1024, "r_xT")
    b_rxt = Buf("rxt")
    o_wg = ar.alloc(DC * 8, "wg")
    b_wg = Buf("wg")
    o_lg = ar.alloc(64, "logits")
    b_lg = Buf("lg")
    o_eg = ar.alloc(2 * NT, "eix_gate")
    b_eg = Buf("eg")
    o_ps5 = ar.alloc(16, "pair_slot")
    b_pair = Buf("pair")
    o_pairs = ar.alloc(4 * NT, "pairs_tmp")
    b_pairs = Buf("pairs")

    LG = AR[:, o_lg:o_lg + 8]
    NM = AR[:, o_lg + 8:o_lg + 9]
    RS = AR[:, o_lg + 9:o_lg + 10]
    RMAXP = AR[:, o_lg + 10:o_lg + 11]
    MSK = AR[:, o_lg + 16:o_lg + 24]
    EXPS = AR[:, o_lg + 24:o_lg + 32]
    KTOK = AR[:, o_lg + 32:o_lg + 33]
    PTOK = AR[:, o_lg + 33:o_lg + 34]
    EF = AR[:, o_lg + 34:o_lg + 35]
    BASE = AR[:, o_lg + 35:o_lg + 36]
    SLOTF = AR[:, o_lg + 36:o_lg + 37]
    SLOTU = AR[:, o_lg + 37:o_lg + 38]
    PM = AR[:, o_lg + 38:o_lg + 39]
    PD = AR[:, o_lg + 39:o_lg + 40]
    R1 = AR[:, o_lg + 40:o_lg + 41]
    MX8R = AR[:, o_lg + 48:o_lg + 56]
    IX8 = AR[:, o_lg + 56:o_lg + 64]
    EIX = AR[:, o_eg:o_eg + NT]
    GATE = AR[:, o_eg + NT:o_eg + 2 * NT]
    PAIR = AR[:, o_ps5:o_ps5 + 2]
    RXH = AR[:, o_rxh:o_rxh + 1024]
    PAIRS = AR[:, o_pairs:o_pairs + 2 * NT]
    TMPM = AR[:, o_pairs + 2 * NT:o_pairs + 3 * NT]
    TMP2 = AR[:, o_pairs + 3 * NT:o_pairs + 4 * NT]

    em("sync", lambda e: e.dma_start(
        out=AR[:, o_wg:o_wg + DC * 8].rearrange("p (c ee) -> p c ee", ee=8),
        in_=dt["wg"].rearrange("(c p) ee -> p c ee", p=128)),
       reads=[b_x1s], writes=[b_wg], dma=True)
    em("vector", lambda e: e.memset(ZERO8, 0.0), writes=[b_s8])
    b_slots_d = Buf("slots_dram")
    em("gpsimd", lambda e: e.dma_start(
        out=dt["slots"][0:N, :].rearrange("(p j) v -> p (j v)", p=128),
        in_=PREFILL),
       reads=[b_consts], writes=[b_slots_d], dma=True)

    for t in range(NT):
        par = t % 2
        xr = AR[:, o_r + par * 1024:o_r + (par + 1) * 1024]
        em("sync", lambda e, t=t, xr=xr: e.dma_start(
            out=xr, in_=dt["x1s"][t * 128:(t + 1) * 128, :]),
           reads=[b_x1s], writes=[r_bufs[par]], dma=True)
        emit_ln(xr, RXH, [r_bufs[par]], [b_rxh])
        emit_transposes(RXH, [b_rxh], o_rxt, [b_rxt])
        pv = psv(2, cols=(0, 8))
        fns = [lambda e, c=c, pv=pv: e.matmul(
            pv, AR[:, o_rxt + c * 128:o_rxt + (c + 1) * 128],
            AR[:, o_wg + c * 8:o_wg + (c + 1) * 8],
            start=(c == 0), stop=(c == DC - 1)) for c in range(DC)]
        em("tensor", fns, reads=[b_rxt, b_wg], writes=[psb(2, 0)])
        b_nm, b_rs, b_msk, b_ix8, b_mx8r = (
            Buf("nm"), Buf("rs"), Buf("msk"), Buf("ix8"), Buf("mx8r"))
        em("vector", lambda e, pv=pv: e.tensor_copy(LG, pv),
           reads=[psb(2, 0)], writes=[b_lg])
        em("vector", lambda e: e.max(out=MX8R, in_=LG),
           reads=[b_lg], writes=[b_mx8r])
        em("vector", lambda e: e.tensor_scalar_mul(NM, MX8R[:, 0:1], -1.0),
           reads=[b_mx8r], writes=[b_nm])
        em("scalar", lambda e: e.activation(EXPS, LG, AF.Exp,
                                            bias=NM, accum_out=RS),
           reads=[b_lg, b_nm], writes=[b_rs])
        em("vector", lambda e, t=t: e.reciprocal(GATE[:, t:t + 1], RS),
           reads=[b_rs], writes=[b_eg])
        em("vector", lambda e: e.max_index(out=IX8.bitcast(U32),
                                           in_max=MX8R, in_values=LG),
           reads=[b_lg, b_mx8r], writes=[b_ix8])
        em("vector", lambda e, t=t: e.tensor_copy(
            EIX[:, t:t + 1].bitcast(U32), IX8[:, 0:1].bitcast(U32)),
           reads=[b_ix8], writes=[b_eg])
        em("vector", lambda e: e.tensor_scalar(out=MSK, in0=LG, scalar1=NM,
                                               scalar2=0.0, op0=ALU.add,
                                               op1=ALU.is_equal),
           reads=[b_lg, b_nm], writes=[b_msk])
        em("tensor", lambda e: e.transpose(PS[3][0:8, 0:128], MSK, IDENT),
           reads=[b_msk, b_consts], writes=[psb(3, 0)])
        em("vector", lambda e, t=t: e.tensor_copy(
            MASKT[:, t * 128:(t + 1) * 128], PS[3][0:8, 0:128]),
           reads=[psb(3, 0)], writes=[b_s8])

    em("vector", lambda e: e.tensor_tensor_scan(
        out=CUMT, data0=MASKT, data1=ZERO8, initial=0.0,
        op0=ALU.add, op1=ALU.add), reads=[b_s8], writes=[b_s8])
    em("vector", lambda e: e.tensor_sub(CUMT, CUMT, MASKT),
       reads=[b_s8], writes=[b_s8])
    em("vector", lambda e: e.tensor_scalar(out=KEEPT, in0=CUMT,
                                           scalar1=float(CAP), scalar2=None,
                                           op0=ALU.is_lt),
       reads=[b_s8], writes=[b_s8])
    em("vector", lambda e: e.tensor_mul(KEEPT, KEEPT, MASKT),
       reads=[b_s8], writes=[b_s8])
    em("vector", lambda e: e.tensor_mul(PKT, CUMT, KEEPT),
       reads=[b_s8], writes=[b_s8])

    tap("maskt", MASKT, [b_s8])
    tap("cumt", CUMT, [b_s8])
    tap("keept", KEEPT, [b_s8])
    for t in range(NT):
        em("tensor", [
            lambda e, t=t: e.transpose(
                PS[3][0:128, 0:8], KEEPT[:, t * 128:(t + 1) * 128],
                IDENT[0:8, 0:8]),
            lambda e, t=t: e.transpose(
                PS[3][0:128, 16:24], PKT[:, t * 128:(t + 1) * 128],
                IDENT[0:8, 0:8]),
        ], reads=[b_s8, b_consts], writes=[psb(3, 0)])
        b_sl = Buf("sl")
        seq = [
            lambda e: e.reduce_sum(KTOK, PS[3][0:128, 0:8], axis=AX.X),
            lambda e: e.reduce_sum(PTOK, PS[3][0:128, 16:24], axis=AX.X),
            lambda e, t=t: e.tensor_copy(EF, EIX[:, t:t + 1].bitcast(U32)),
            lambda e: e.tensor_single_scalar(PD, PTOK, 128.0, ALU.is_ge),
            lambda e: e.tensor_single_scalar(PM, PTOK, 256.0, ALU.is_ge),
            lambda e: e.tensor_add(PD, PD, PM),
            lambda e: e.tensor_single_scalar(PM, PTOK, 384.0, ALU.is_ge),
            lambda e: e.tensor_add(PD, PD, PM),
            lambda e: e.tensor_scalar_mul(PM, PD, -4095.0),
            lambda e: e.scalar_tensor_tensor(out=R1, in0=PTOK, scalar=32.0,
                                             in1=PM, op0=ALU.mult, op1=ALU.add),
            lambda e: e.tensor_scalar(out=BASE, in0=EF, scalar1=4.0,
                                      scalar2=R1, op0=ALU.mult, op1=ALU.add),
            lambda e, t=t: e.tensor_sub(BASE, BASE, ALT[:, t:t + 1]),
            lambda e, t=t: e.scalar_tensor_tensor(
                out=SLOTF, in0=BASE, scalar=KTOK, in1=ALT[:, t:t + 1],
                op0=ALU.mult, op1=ALU.add),
            lambda e: e.tensor_copy(SLOTU.bitcast(U32), SLOTF),
            lambda e, t=t: e.tensor_copy(PAIR[:, 0:1], TOKID[:, t:t + 1]),
            lambda e, t=t: e.tensor_copy(PAIR[:, 1:2], GATE[:, t:t + 1]),
        ]
        for fi, f in enumerate(seq):
            em("vector", f, reads=[psb(3, 0), b_eg, b_consts, b_sl],
               writes=[b_pair if fi >= len(seq) - 3 else b_sl])
        em("gpsimd", lambda e: e.indirect_dma_start(
            out=dt["slots"][:, :],
            out_offset=IndirectOffsetOnAxis(ap=SLOTU.bitcast(U32), axis=0),
            in_=PAIR, in_offset=None),
           reads=[b_pair, b_slots_d], writes=[b_slots_d], dma=True)

    em("gpsimd", lambda e: e.dma_start(
        out=PAIRS,
        in_=dt["slots"][0:N, :].rearrange("(p j) v -> p (j v)", p=128)),
       reads=[b_slots_d], writes=[b_pairs], dma=True)
    PAIRS3 = PAIRS.rearrange("p (j v) -> p j v", v=2)
    b_rb = Buf("rb")
    for f in [
        lambda e: e.tensor_scalar_max(TMPM, PAIRS3[:, :, 0:1], 0.0),
        lambda e: e.tensor_copy(TIDG.bitcast(U32), TMPM),
        lambda e: e.tensor_copy(GSLOT, PAIRS3[:, :, 1:2]),
        lambda e: e.tensor_scalar(out=TMPM, in0=PAIRS3[:, :, 0:1], scalar1=0.0,
                                  scalar2=None, op0=ALU.is_ge),
        lambda e: e.tensor_sub(TMP2, PAIRS3[:, :, 0:1], ALTR),
        lambda e: e.tensor_mul(TMP2, TMP2, TMPM),
        lambda e: e.tensor_add(TMP2, TMP2, ALTR),
        lambda e: e.tensor_copy(TIDS.bitcast(U32), TMP2),
    ]:
        em("vector", f, reads=[b_pairs, b_consts, b_rb],
           writes=[b_tid, b_rb])

    tap("tidg", TIDG, [b_tid])
    tap("tids", TIDS, [b_tid])
    tap("gslot", GSLOT, [b_tid])
    tap("eix", EIX, [b_eg])
    tap("gate", GATE, [b_eg])

    if PHASES < 7:
        return _finish()
    bld.barrier()
    ar.reset()
    o_ht = ar.alloc(32 * 512, "hT")
    b_ht = Buf("hT")
    o_xe = ar.alloc(4 * 1024, "Xe")
    xe_bufs = [Buf(f"xe{j}") for j in range(4)]
    o_xet = ar.alloc(DC * 512, "XeT")
    b_xet = Buf("XeT")
    o_w1 = ar.alloc(2 * 1024, "w1buf")
    w1_bufs = [Buf("w1a"), Buf("w1b")]
    o_w2 = ar.alloc(2 * 1024, "w2buf")
    w2_bufs = [Buf("w2a"), Buf("w2b")]
    o_oe = ar.alloc(4 * 1024, "Oe")
    b_oe = Buf("Oe")

    def ht_chunk(i):
        return AR[:, o_ht + i * 512:o_ht + (i + 1) * 512]

    for ex in range(min(E, NEXPERTS)):
        for j in range(4):
            xe = AR[:, o_xe + j * 1024:o_xe + (j + 1) * 1024]
            em("gpsimd", lambda e, ex=ex, j=j, xe=xe: e.indirect_dma_start(
                out=xe, out_offset=None, in_=dt["x1s"][0:N, :],
                in_offset=IndirectOffsetOnAxis(
                    ap=TIDG[:, ex * 4 + j:ex * 4 + j + 1].bitcast(U32),
                    axis=0)),
               reads=[b_tid, b_x1s], writes=[xe_bufs[j]], dma=True)
            emit_ln(xe, xe, [xe_bufs[j]], [xe_bufs[j]])
            if P7STEPS < 2:
                continue
            for c in range(DC):
                ph = c % 2
                pv = psv(2, cols=(ph * 512, ph * 512 + 128))
                em("tensor", lambda e, c=c, xe=xe, pv=pv: e.transpose(
                    pv, xe[:, c * 128:(c + 1) * 128], IDENT),
                   reads=[xe_bufs[j], b_consts], writes=[psb(2, ph)])
                em("scalar", lambda e, c=c, j=j, pv=pv: e.copy(
                    AR[:, o_xet + c * 512 + j * 128:
                       o_xet + c * 512 + (j + 1) * 128], pv),
                   reads=[psb(2, ph)], writes=[b_xet])
        if P7STEPS < 3:
            continue
        for hfc in range(32):
            par = hfc % 2
            w1 = AR[:, o_w1 + par * 1024:o_w1 + (par + 1) * 1024]
            em("sync", lambda e, ex=ex, hfc=hfc, w1=w1: e.dma_start(
                out=w1, in_=dt["w1c"][ex, hfc]),
               writes=[w1_bufs[par]], dma=True)
            pv = psv(3, par)
            if FFN1_MODE != 1:
                fns = [lambda e, c=c, w1=w1, pv=pv: e.matmul(
                    pv, w1[:, c * 128:(c + 1) * 128],
                    AR[:, o_xet + c * 512:o_xet + (c + 1) * 512],
                    start=(c == 0), stop=(c == DC - 1)) for c in range(DC)]
                em("tensor", fns, reads=[w1_bufs[par], b_xet],
                   writes=[psb(3, par)])
            if FFN1_MODE == 2:
                em("vector", lambda e, hfc=hfc, pv=pv: e.tensor_copy(
                    ht_chunk(hfc), pv),
                   reads=[psb(3, par)], writes=[b_ht])
            elif FFN1_MODE == 0:
                em("scalar", lambda e, hfc=hfc, pv=pv: e.activation(
                    ht_chunk(hfc), pv, AF.Gelu_apprx_tanh),
                   reads=[psb(3, par)], writes=[b_ht])
        if P7STEPS < 4:
            continue
        for hfc in range(32):
            par = hfc % 2
            w2 = AR[:, o_w2 + par * 1024:o_w2 + (par + 1) * 1024]
            em("sync", lambda e, ex=ex, hfc=hfc, w2=w2: e.dma_start(
                out=w2, in_=dt["w2"][ex, hfc * 128:(hfc + 1) * 128, :]),
               writes=[w2_bufs[par]], dma=True)
            fns = [lambda e, g=g, hfc=hfc, w2=w2: e.matmul(
                psv(g // 2, g % 2),
                ht_chunk(hfc)[:, (g // 2) * 128:(g // 2 + 1) * 128],
                w2[:, (g % 2) * 512:(g % 2 + 1) * 512],
                start=(hfc == 0), stop=(hfc == 31)) for g in range(8)]
            em("tensor", fns, reads=[w2_bufs[par], b_ht],
               writes=[psb(g // 2, g % 2) for g in range(8)])
        for g in range(8):
            t, dh = g // 2, g % 2
            em("vector", lambda e, ex=ex, t=t, dh=dh, g=g: e.tensor_scalar_mul(
                AR[:, o_oe + t * 1024 + dh * 512:
                   o_oe + t * 1024 + (dh + 1) * 512],
                psv(g // 2, g % 2), GSLOT[:, ex * 4 + t:ex * 4 + t + 1]),
               reads=[psb(g // 2, g % 2), b_tid], writes=[b_oe])
        if ex == 0:
            tap("xet0", AR[:, o_xet:o_xet + 512], [b_xet])
            tap("ht0", ht_chunk(0), [b_ht])
            tap("oe0", AR[:, o_oe:o_oe + 1024], [b_oe])
        if P7STEPS < 5:
            continue
        for j in range(4):
            em("gpsimd", lambda e, ex=ex, j=j: e.indirect_dma_start(
                out=dt["x1s"][:, :],
                out_offset=IndirectOffsetOnAxis(
                    ap=TIDS[:, ex * 4 + j:ex * 4 + j + 1].bitcast(U32),
                    axis=0),
                in_=AR[:, o_oe + j * 1024:o_oe + (j + 1) * 1024],
                in_offset=None, compute_op=ALU.add),
               reads=[b_oe, b_tid], writes=[b_x1s], dma=True)

    if PHASES < 8:
        return _finish()
    b_out = Buf("out")
    for t in range(NT):
        em("sync", lambda e, t=t: e.dma_start(
            out=dt["out"][t * 128:(t + 1) * 128, :],
            in_=dt["x1s"][t * 128:(t + 1) * 128, :]),
           reads=[b_x1s], writes=[b_out], dma=True)
    return _finish()


_KF_ROWS = np.concatenate([np.arange(f * 128, (f + 1) * 128)
                           for f in (0, 8, 16, 24)])


def kernel(L0_attn_x, L0_attn_W_qkv, L0_attn_b_qkv, L0_attn_W_proj,
           L0_attn_b_proj, L0_attn_clusters, L0_attn_keyframes,
           L0_ffn_Wg, L0_ffn_W1, L0_ffn_W2):
    x = np.ascontiguousarray(np.asarray(L0_attn_x, np.float32))
    clusters = np.asarray(L0_attn_clusters)
    keyframes = np.asarray(L0_attn_keyframes)
    assert np.array_equal(
        clusters, np.arange(32, dtype=clusters.dtype).reshape(4, 8))
    assert np.array_equal(keyframes, clusters[:, 0])

    kfx = np.ascontiguousarray(x[:, _KF_ROWS, :].reshape(B * 512, D))
    w1 = np.ascontiguousarray(np.asarray(L0_ffn_W1, np.float32))
    w1c = np.ascontiguousarray(
        w1.reshape(E, DC, 128, 32, 128).transpose(0, 3, 2, 1, 4)
        .reshape(E, 32, 128, DC * 128))

    nc = build_program()
    shared = {
        "kfx": kfx,
        "wqkv": np.ascontiguousarray(np.asarray(L0_attn_W_qkv, np.float32)),
        "bqkv": np.ascontiguousarray(np.asarray(L0_attn_b_qkv, np.float32)),
        "wproj": np.ascontiguousarray(np.asarray(L0_attn_W_proj, np.float32)),
        "bproj": np.ascontiguousarray(np.asarray(L0_attn_b_proj, np.float32)),
        "wg": np.ascontiguousarray(np.asarray(L0_ffn_Wg, np.float32)),
        "w1c": w1c,
        "w2": np.ascontiguousarray(np.asarray(L0_ffn_W2, np.float32)),
        "consts": _make_consts(),
    }
    in_maps = [dict(shared, x=np.ascontiguousarray(x[b])) for b in range(B)]
    res = run_bass_kernel_spmd(nc, in_maps, core_ids=list(range(B)))
    return np.stack([res.results[b]["out"] for b in range(B)], axis=0)


# version 04
# speedup vs baseline: 1.1282x; 1.0470x over previous
import contextlib
import numpy as np
import concourse.bass as bass
import concourse.mybir as mybir
from concourse.bass_utils import run_bass_kernel_spmd
from concourse.bass_core_rust import IndirectOffsetOnAxis

F32 = mybir.dt.float32
U32 = mybir.dt.uint32
I32 = mybir.dt.int32
AF = mybir.ActivationFunctionType
ALU = mybir.AluOpType
AX = mybir.AxisListType

B, N, D, H, DH = 8, 4096, 1024, 16, 64
NT, DC = N // 128, D // 128
KCL, FC = 4, 8
TK, NSUB = 38, 5
GA, GB = NSUB * TK, (FC - NSUB) * TK
G = KCL * (GA + GB)
GP, GT = 1280, 10
CAP, E, HF = 512, 8, 4096
SCALE = DH ** -0.5
EPS = 1e-5
ARENA_COLS = 52992


def _cluster_chunks():
    out = []
    for ci in range(KCL):
        need = np.zeros(GP, bool)
        need[0:4 * GA + GB] = True
        if ci > 0:
            s = 4 * GA + GB * ci
            need[s:s + GB] = True
        chunks = []
        for c in range(GT):
            seg = need[c * 128:(c + 1) * 128]
            if seg.any():
                chunks.append((c, None if seg.all() else seg.astype(np.float32)))
        out.append(chunks)
    return out


CLUSTER_CHUNKS = _cluster_chunks()
N_MASKS = sum(1 for ch in CLUSTER_CHUNKS for (_, m) in ch if m is not None)


FCONSTS = [EPS, 1.0 / D, SCALE]


def _const_layout():
    off, o = {}, 0
    for name, n in (("ident", 128), ("ones_row", 128), ("masks", max(N_MASKS, 1)),
                    ("alt", NT), ("tokid", NT), ("altr", NT), ("basenc", 1),
                    ("prefill", 64), ("fconst", len(FCONSTS))):
        off[name] = o
        o += n
    return off, o


CONST_OFF, CONST_COLS = _const_layout()


def _make_consts():
    c = np.zeros((128, CONST_COLS), np.float32)
    o = CONST_OFF
    c[:, o["ident"]:o["ident"] + 128] = np.eye(128, dtype=np.float32)
    c[0, o["ones_row"]:o["ones_row"] + 128] = 1.0
    mi = 0
    for ch in CLUSTER_CHUNKS:
        for (_, m) in ch:
            if m is not None:
                c[:, o["masks"] + mi] = m
                mi += 1
    tok = (np.arange(128)[:, None] + 128 * np.arange(NT)[None, :]).astype(np.float32)
    c[:, o["tokid"]:o["tokid"] + NT] = tok
    c[:, o["alt"]:o["alt"] + NT] = tok + 4096.0
    rowid = (32 * np.arange(128)[:, None] + np.arange(NT)[None, :]).astype(np.float32)
    c[:, o["altr"]:o["altr"] + NT] = rowid + 4096.0
    c[0:4, o["basenc"]] = np.arange(4, dtype=np.float32) * 1024.0
    pf = np.zeros((128, 64), np.float32)
    pf[:, 0::2] = -1.0
    c[:, o["prefill"]:o["prefill"] + 64] = pf
    for i, v in enumerate(FCONSTS):
        c[:, o["fconst"] + i] = np.float32(v)
    return c


MASK_COL = {}
_mi = 0
for _ci, _ch in enumerate(CLUSTER_CHUNKS):
    for (_c, _m) in _ch:
        if _m is not None:
            MASK_COL[(_ci, _c)] = _mi
            _mi += 1


class Sem:
    def __init__(self, stack, nc, name):
        self.h = stack.enter_context(nc.semaphore(name))
        self.n = 0

    def inc(self, k=1):
        self.n += k
        return self.n


class Buf:
    __slots__ = ("name", "ready", "readers", "dsem_hw", "dsem_sw")

    def __init__(self, name=""):
        self.name = name
        self.ready = None
        self.readers = {}
        self.dsem_hw = None
        self.dsem_sw = None


class Arena:
    def __init__(self):
        self.off = 0
        self.marks = []

    def alloc(self, cols, name=""):
        o = (self.off + 15) // 16 * 16
        assert o + cols <= ARENA_COLS, f"arena overflow at {name}: {o}+{cols}"
        self.off = o + cols
        return o

    def mark(self):
        self.marks.append(self.off)

    def reset(self):
        self.off = self.marks.pop()


class Builder:

    ENGS = ("sync", "tensor", "vector", "scalar", "gpsimd")

    def __init__(self, nc, stack):
        self.nc = nc
        self.stack = stack
        self.streams = {e: [] for e in self.ENGS}
        self.comp = {e: Sem(stack, nc, f"c_{e}") for e in self.ENGS}
        self.dsems = []

    def _dsem(self, buf, sw):
        attr = "dsem_sw" if sw else "dsem_hw"
        s = getattr(buf, attr)
        if s is None:
            s = Sem(self.stack, self.nc, f"ds{len(self.dsems)}")
            setattr(buf, attr, s)
            self.dsems.append(s)
        return s

    def emit(self, eng, fns, reads=(), writes=(), dma=False):
        if callable(fns):
            fns = [fns]
        if dma:
            assert len(writes) >= 1
            sem = self._dsem(writes[0], sw=(eng == "gpsimd"))
        else:
            sem = self.comp[eng]
        waits = {}

        def add(sv):
            if sv is None:
                return
            s, v = sv
            if waits.get(s, 0) < v:
                waits[s] = v

        for b in reads:
            add(b.ready)
        for b in writes:
            if not (dma and b.ready is not None and b.ready[0] is sem
                    and not b.readers):
                add(b.ready)
            for s, v in b.readers.items():
                add((s, v))
        k = 16 if dma else 1
        tgt = sem.inc(k)
        for b in writes:
            b.ready = (sem, tgt)
            b.readers = {}
        for b in reads:
            b.readers[sem] = tgt
        wl = list(waits.items())

        def run(e, wl=wl, fns=fns, sem=sem, k=k):
            for s, v in wl:
                e.wait_ge(s.h, v)
            ins = None
            for f in fns:
                ins = f(e)
            ins.then_inc(sem.h, k)

        self.streams[eng].append(run)

    def barrier(self):
        targets = ([(s, s.n) for s in self.comp.values() if s.n > 0]
                   + [(s, s.n) for s in self.dsems if s.n > 0])
        for e in self.ENGS:
            own = self.comp[e]
            wl = [(s, v) for s, v in targets if s is not own]

            def run(eng, wl=wl):
                for s, v in wl:
                    eng.wait_ge(s.h, v)

            self.streams[e].append(run)

    def final_drain(self, eng):
        sems = [(s, s.n) for s in self.dsems]

        def run(e, sems=sems):
            for s, v in sems:
                e.wait_ge(s.h, v)

        self.streams[eng].append(run)


DBG_TAPS = set()
PHASES = 99
NEXPERTS = E
P7STEPS = 5
FFN1_MODE = 0


def build_program(dbg=False):
    nc = bass.Bass("TRN2", target_bir_lowering=False)
    stack = contextlib.ExitStack()
    scratch_kind = "ExternalOutput" if dbg else "Internal"

    dt = {}
    dt["x"] = nc.dram_tensor("x", [N, D], F32, kind="ExternalInput").ap()
    dt["kfx"] = nc.dram_tensor("kfx", [B * 512, D], F32, kind="ExternalInput").ap()
    dt["wqkv"] = nc.dram_tensor("wqkv", [D, 3 * D], F32, kind="ExternalInput").ap()
    dt["bqkv"] = nc.dram_tensor("bqkv", [3 * D], F32, kind="ExternalInput").ap()
    dt["wproj"] = nc.dram_tensor("wproj", [D, D], F32, kind="ExternalInput").ap()
    dt["bproj"] = nc.dram_tensor("bproj", [D], F32, kind="ExternalInput").ap()
    dt["wg"] = nc.dram_tensor("wg", [D, E], F32, kind="ExternalInput").ap()
    dt["w1c"] = nc.dram_tensor("w1c", [E, 32, 128, DC * 128], F32,
                               kind="ExternalInput").ap()
    dt["w2"] = nc.dram_tensor("w2", [E, HF, D], F32, kind="ExternalInput").ap()
    dt["consts"] = nc.dram_tensor("consts", [128, CONST_COLS], F32,
                                  kind="ExternalInput").ap()
    dt["out"] = nc.dram_tensor("out", [N, D], F32, kind="ExternalOutput").ap()
    dt["x1s"] = nc.dram_tensor("x1s", [2 * N, D], F32, kind=scratch_kind).ap()
    dt["gdram"] = nc.dram_tensor("gdram", [GP], F32, kind=scratch_kind).ap()
    dt["slots"] = nc.dram_tensor("slots", [2 * N, 2], F32, kind=scratch_kind).ap()

    arena_t = stack.enter_context(nc.sbuf_tensor("arena", [128, ARENA_COLS], F32))
    AR = arena_t[:, :]
    psum_t = [stack.enter_context(nc.psum_tensor(f"ps{i}", [128, 1024], F32))
              for i in range(4)]
    PS = [p[:, :] for p in psum_t]
    ps_buf = [Buf(f"ps{i}{h}") for i in range(4) for h in (0, 1)]

    def psv(i, half=None, cols=None):
        if cols is not None:
            return PS[i][:, cols[0]:cols[1]]
        return PS[i][:, half * 512:(half + 1) * 512]

    def psb(i, half):
        return ps_buf[2 * i + half]

    bld = Builder(nc, stack)
    ar = Arena()
    em = bld.emit
    TS, VS, SS = bld.comp["tensor"], bld.comp["vector"], bld.comp["scalar"]
    CO = CONST_OFF

    o_consts = ar.alloc(CONST_COLS, "consts")
    o_bqkvb = ar.alloc(3 * D, "bqkv_bcast")
    o_bprojb = ar.alloc(D, "bproj_bcast")
    o_bqs = ar.alloc(16, "bq_bk_cols")
    o_small = ar.alloc(128, "smalls")
    o_lnt = ar.alloc(16, "ln_tmp")
    o_lns = ar.alloc(1024, "ln_scratch")
    b_consts, b_bqkvb, b_bprojb, b_bqs, b_small = (
        Buf(x) for x in ("consts", "bqkvb", "bprojb", "bqs", "small"))

    CONSTS = AR[:, o_consts:o_consts + CONST_COLS]
    IDENT = CONSTS[:, CO["ident"]:CO["ident"] + 128]
    ONES_ROW = AR[0:1, o_consts + CO["ones_row"]:o_consts + CO["ones_row"] + 128]
    TOKID = CONSTS[:, CO["tokid"]:CO["tokid"] + NT]
    ALT = CONSTS[:, CO["alt"]:CO["alt"] + NT]
    ALTR = CONSTS[:, CO["altr"]:CO["altr"] + NT]
    BASENC = AR[0:4, o_consts + CO["basenc"]:o_consts + CO["basenc"] + 1]
    PREFILL = CONSTS[:, CO["prefill"]:CO["prefill"] + 64]
    BQKVB = AR[:, o_bqkvb:o_bqkvb + 3 * D]
    BPROJB = AR[:, o_bprojb:o_bprojb + D]
    BQC = AR[:, o_bqs:o_bqs + 8]
    BKC = AR[:, o_bqs + 8:o_bqs + 16]
    LNS = AR[:, o_lns:o_lns + 1024]
    GSBF = AR[:, o_small:o_small + GT]
    GSBU = AR[:, o_small + 16:o_small + 16 + GT]

    for i, v in enumerate(FCONSTS):
        fo = o_consts + CO["fconst"] + i
        nc.const_aps.register(F32, v, AR[:, fo:fo + 1])

    def _finish():
        bld.final_drain("sync")
        with nc.Block() as block:
            @block.sync
            def _(e):
                for f in bld.streams["sync"]:
                    f(e)

            @block.tensor
            def _(e):
                for f in bld.streams["tensor"]:
                    f(e)

            @block.vector
            def _(e):
                for f in bld.streams["vector"]:
                    f(e)

            @block.scalar
            def _(e):
                for f in bld.streams["scalar"]:
                    f(e)

            @block.gpsimd
            def _(e):
                for f in bld.streams["gpsimd"]:
                    f(e)

        stack.close()
        return nc

    _tapn = [0]

    def tap(name, ap, rbufs):
        if name not in DBG_TAPS:
            return
        _tapn[0] += 1
        d = nc.dram_tensor(f"tap_{name}", list(ap.shape), F32,
                           kind="ExternalOutput").ap()
        em("scalar", lambda e, d=d, ap=ap: e.dma_start(out=d, in_=ap),
           reads=list(rbufs), writes=[Buf("tap")], dma=True)

    def mask_col(ci, c):
        mi = MASK_COL.get((ci, c))
        if mi is None:
            return None
        return CONSTS[:, CO["masks"] + mi:CO["masks"] + mi + 1]

    def emit_ln(xin, xout, rbufs, wbufs):
        m0 = AR[:, o_lnt:o_lnt + 1]
        m = AR[:, o_lnt + 1:o_lnt + 2]
        ssq = AR[:, o_lnt + 2:o_lnt + 3]
        srt = AR[:, o_lnt + 3:o_lnt + 4]
        rstd = AR[:, o_lnt + 4:o_lnt + 5]
        t0, t1, t2, t3 = Buf("ln0"), Buf("ln1"), Buf("ln2"), Buf("ln3")
        em("vector", lambda e: e.reduce_sum(m0, xin, axis=AX.X),
           reads=list(rbufs), writes=[t0])
        em("vector", lambda e: e.tensor_scalar_mul(m, m0, -1.0 / D),
           reads=[t0], writes=[t1])
        em("vector", lambda e: e.tensor_scalar(out=xout, in0=xin, scalar1=m,
                                               scalar2=None, op0=ALU.add),
           reads=list(rbufs) + [t1], writes=list(wbufs))
        em("vector", lambda e: e.scalar_tensor_tensor(
            out=LNS, in0=xout, scalar=1.0, in1=xout, op0=ALU.mult,
            op1=ALU.mult, accum_out=ssq),
           reads=list(wbufs), writes=[t2, Buf("lns")])
        em("scalar", lambda e: e.activation(srt, ssq, AF.Sqrt,
                                            bias=EPS, scale=1.0 / D),
           reads=[t2, b_consts], writes=[t3])
        em("vector", lambda e: e.reciprocal(rstd, srt),
           reads=[t3], writes=[t3])
        em("vector", lambda e: e.tensor_scalar_mul(xout, xout, rstd),
           reads=[t3], writes=list(wbufs))

    em("sync", lambda e: e.dma_start(out=CONSTS, in_=dt["consts"][:, :]),
       writes=[b_consts], dma=True)
    ar.mark()
    o_brow = ar.alloc(4096, "bias_rows")
    b_brow = Buf("brow")
    em("sync", lambda e: e.dma_start(
        out=AR[0:1, o_brow:o_brow + 3 * D],
        in_=dt["bqkv"].rearrange("(a d) -> a d", a=1)),
       writes=[b_brow], dma=True)
    em("sync", lambda e: e.dma_start(
        out=AR[0:1, o_brow + 3 * D:o_brow + 4 * D],
        in_=dt["bproj"].rearrange("(a d) -> a d", a=1)),
       writes=[b_brow], dma=True)
    em("sync", lambda e: e.dma_start(
        out=BQC, in_=dt["bqkv"][0:1024].rearrange("(c p) -> p c", p=128),
        allow_slow_non_contiguous=True),
       writes=[b_bqs], dma=True)
    em("sync", lambda e: e.dma_start(
        out=BKC, in_=dt["bqkv"][1024:2048].rearrange("(c p) -> p c", p=128),
        allow_slow_non_contiguous=True),
       writes=[b_bqs], dma=True)
    em("vector", lambda e: e.tensor_scalar_mul(BQC, BQC, SCALE),
       reads=[b_consts], writes=[b_bqs])
    for j in range(8):
        src = AR[0:1, o_brow + j * 512:o_brow + (j + 1) * 512]
        dst = (BQKVB[:, j * 512:(j + 1) * 512] if j < 6
               else BPROJB[:, (j - 6) * 512:(j - 5) * 512])
        pv = psv(0, j % 2)
        em("tensor", lambda e, s=src, p=pv: e.matmul(p, ONES_ROW, s,
                                                     start=True, stop=True),
           reads=[b_brow, b_consts], writes=[psb(0, j % 2)])
        em("vector", lambda e, p=pv, d=dst: e.tensor_copy(d, p),
           reads=[psb(0, j % 2)], writes=[b_bqkvb if j < 6 else b_bprojb])
    tap("bqkvb", BQKVB, [b_bqkvb])
    tap("bprojb", BPROJB, [b_bprojb])
    tap("bqc", AR[:, o_bqs:o_bqs + 16], [b_bqs])

    bld.barrier()
    ar.reset()
    ar.mark()
    o_wqk = ar.alloc(DC * 2048, "wqk")
    o_kf = ar.alloc(2 * 1024, "kf_tiles")
    o_xh = ar.alloc(1024, "p2_xhat")
    o_xt = ar.alloc(1024, "p2_xT")
    o_qk = ar.alloc(2048, "p2_qk")
    o_pr = ar.alloc(1024, "p2_prod")
    o_s16 = ar.alloc(32, "p2_s16")
    o_sacc = ar.alloc(16, "p2_sacc")
    o_sc = ar.alloc(768, "p2_score")
    b_wqk, b_kf0, b_kf1, b_xh, b_xt, b_qk, b_pr, b_sacc, b_s16 = (
        Buf(x) for x in ("wqk", "kf0", "kf1", "xh", "xt", "qk", "pr", "sacc",
                         "s16"))
    kf_bufs = [b_kf0, b_kf1]
    SACC = AR[:, o_sacc:o_sacc + 4]
    XH = AR[:, o_xh:o_xh + 1024]
    QKT = AR[:, o_qk:o_qk + 2048]
    PRD = AR[:, o_pr:o_pr + 1024]
    S16 = AR[:, o_s16:o_s16 + 16]
    S1 = AR[:, o_s16 + 16:o_s16 + 17]

    for c in range(DC):
        em("sync", lambda e, c=c: e.dma_start(
            out=AR[:, o_wqk + c * 2048:o_wqk + (c + 1) * 2048],
            in_=dt["wqkv"][c * 128:(c + 1) * 128, 0:2048]),
           writes=[b_wqk, b_brow], dma=True)
    em("vector", lambda e: e.memset(SACC, 0.0), writes=[b_sacc])

    def emit_transposes(src_ap, src_bufs, dst_off, dst_bufs, nb=8):
        for j in range(nb):
            ph = j % 2
            pv = psv(2, cols=(ph * 512, ph * 512 + 128))
            dst = AR[:, dst_off + j * 128:dst_off + (j + 1) * 128]
            db = dst_bufs[j % len(dst_bufs)]
            em("tensor", lambda e, j=j, p=pv, s=src_ap: e.transpose(
                p, s[:, j * 128:(j + 1) * 128], IDENT),
               reads=list(src_bufs) + [b_consts], writes=[psb(2, ph)])
            em("scalar", lambda e, p=pv, d=dst: e.copy(d, p),
               reads=[psb(2, ph)], writes=[db])

    for b in range(B):
        for kfi in range(4):
            ti = b * 4 + kfi
            par = ti % 2
            kf_ap = AR[:, o_kf + par * 1024:o_kf + (par + 1) * 1024]
            em("sync", lambda e, r=ti * 128, d=kf_ap: e.dma_start(
                out=d, in_=dt["kfx"][r:r + 128, :]),
               writes=[kf_bufs[par]], dma=True)
            emit_ln(kf_ap, XH, [kf_bufs[par]], [b_xh])
            emit_transposes(XH, [b_xh], o_xt, [b_xt])
            for g in range(4):
                pi, ph = g // 2, g % 2
                pv = psv(pi, ph)
                fns = [lambda e, c=c, g=g, pv=pv: e.matmul(
                    pv, AR[:, o_xt + c * 128:o_xt + (c + 1) * 128],
                    AR[:, o_wqk + c * 2048 + g * 512:o_wqk + c * 2048 + (g + 1) * 512],
                    start=(c == 0), stop=(c == DC - 1)) for c in range(DC)]
                em("tensor", fns, reads=[b_xt, b_wqk], writes=[psb(pi, ph)])
                em("vector", lambda e, g=g, pv=pv: e.tensor_add(
                    QKT[:, g * 512:(g + 1) * 512], pv,
                    BQKVB[:, g * 512:(g + 1) * 512]),
                   reads=[psb(pi, ph), b_bqkvb], writes=[b_qk])
            em("vector", lambda e: e.tensor_mul(PRD, QKT[:, 0:1024],
                                                QKT[:, 1024:2048]),
               reads=[b_qk], writes=[b_pr])
            em("vector", lambda e: e.reduce_sum(
                S16, PRD.rearrange("p (h d) -> p h d", h=H), axis=AX.X),
               reads=[b_pr], writes=[b_s16])
            em("vector", lambda e: e.reduce_max(S1, S16, axis=AX.X),
               reads=[b_s16], writes=[b_s16])
            em("vector", lambda e, kfi=kfi: e.tensor_add(
                SACC[:, kfi:kfi + 1], SACC[:, kfi:kfi + 1], S1),
               reads=[b_s16], writes=[b_sacc])
            if ti == 0:
                tap("xh0", XH, [b_xh])
                tap("xt0", AR[:, o_xt:o_xt + 1024], [b_xt])
                tap("qk0", QKT, [b_qk])
                tap("s16_0", S16, [b_sacc])

    SCORE = AR[0:4, o_sc:o_sc + 128]
    WRK = AR[0:4, o_sc + 128:o_sc + 256]
    MX8 = AR[0:4, o_sc + 256:o_sc + 264]
    PIDXU = AR[0:4, o_sc + 272:o_sc + 312]
    PIDXF = AR[0:4, o_sc + 320:o_sc + 360]
    G4 = AR[0:4, o_sc + 368:o_sc + 672]
    ZROW = AR[0:1, o_sc + 688:o_sc + 752]
    b_score, b_wrk, b_mx8, b_pidx, b_g4, b_zrow = (
        Buf(x) for x in ("score", "wrk", "mx8", "pidx", "g4", "zrow"))
    em("tensor", lambda e: e.transpose(PS[2][0:4, 0:128], SACC, IDENT),
       reads=[b_sacc, b_consts], writes=[psb(2, 0)])
    em("scalar", lambda e: e.mul(SCORE, PS[2][0:4, 0:128], 1.0 / B),
       reads=[psb(2, 0), b_consts], writes=[b_score])
    tap("sacc", SACC, [b_sacc])
    tap("score", SCORE, [b_score])
    em("vector", lambda e: e.tensor_copy(WRK, SCORE),
       reads=[b_score], writes=[b_wrk])
    for r in range(5):
        em("vector", lambda e: e.max(out=MX8, in_=WRK),
           reads=[b_wrk], writes=[b_mx8])
        em("vector", lambda e, r=r: e.max_index(
            out=PIDXU[:, r * 8:(r + 1) * 8].bitcast(U32),
            in_max=MX8, in_values=WRK),
           reads=[b_wrk, b_mx8], writes=[b_pidx])
        em("vector", lambda e: e.match_replace(out=WRK, in_to_replace=MX8,
                                               in_values=WRK, imm_value=-1e30),
           reads=[b_mx8], writes=[b_wrk])
    tap("wrk", WRK, [b_wrk])
    tap("pidxu", PIDXU, [b_pidx])
    em("vector", lambda e: e.tensor_copy(PIDXF, PIDXU.bitcast(U32)),
       reads=[b_pidx], writes=[b_pidx])
    tap("pidxf", PIDXF, [b_pidx])
    fns = [lambda e, fl=fl: e.tensor_scalar(
        out=G4[:, fl * TK:(fl + 1) * TK], in0=PIDXF[:, 0:TK],
        scalar1=BASENC, scalar2=float(128 * fl), op0=ALU.add, op1=ALU.add)
        for fl in range(FC)]
    em("vector", fns, reads=[b_pidx, b_consts], writes=[b_g4])
    b_gd = Buf("gdram")
    em("gpsimd", lambda e: e.dma_start(out=dt["gdram"][0:4 * GA],
                                       in_=G4[:, 0:GA]),
       reads=[b_g4], writes=[b_gd], dma=True)
    em("gpsimd", lambda e: e.dma_start(out=dt["gdram"][4 * GA:G],
                                       in_=G4[:, GA:GA + GB]),
       reads=[b_g4], writes=[b_gd], dma=True)
    em("vector", lambda e: e.memset(ZROW, 0.0), writes=[b_zrow])
    em("gpsimd", lambda e: e.dma_start(out=dt["gdram"][G:GP], in_=ZROW),
       reads=[b_zrow], writes=[b_gd], dma=True)
    em("gpsimd", lambda e: e.dma_start(
        out=GSBF, in_=dt["gdram"].rearrange("(c p) -> p c", p=128),
        allow_slow_non_contiguous=True),
       reads=[b_gd], writes=[b_small], dma=True)
    em("vector", lambda e: e.tensor_copy(GSBU.bitcast(U32), GSBF),
       reads=[b_small], writes=[b_small])
    ar.reset()

    if PHASES < 3:
        return _finish()
    bld.barrier()
    ar.mark()
    o_xgt = ar.alloc(DC * GP, "XgT")
    o_kt = ar.alloc(DC * GP, "KT")
    o_v65 = ar.alloc(GT * H * 65, "V65")
    ar.mark()
    o_xg = ar.alloc(2 * 1024, "Xg")
    o_wkv = ar.alloc(2 * 1024, "wkv")
    b_xgt, b_kt, b_v65 = Buf("XgT"), Buf("KT"), Buf("V65")
    xg_bufs = [Buf("xg0"), Buf("xg1")]
    wkv_bufs = [Buf("wkv0"), Buf("wkv1")]

    def xgt_chunk(c):
        return AR[:, o_xgt + c * GP:o_xgt + (c + 1) * GP]

    def kt_chunk(c):
        return AR[:, o_kt + c * GP:o_kt + (c + 1) * GP]

    def v65_tile(t):
        return AR[:, o_v65 + t * H * 65:o_v65 + (t + 1) * H * 65]

    for t in range(GT):
        par = t % 2
        xg = AR[:, o_xg + par * 1024:o_xg + (par + 1) * 1024]
        em("gpsimd", lambda e, t=t, xg=xg: e.indirect_dma_start(
            out=xg, out_offset=None, in_=dt["x"][:, :],
            in_offset=IndirectOffsetOnAxis(
                ap=GSBU[:, t:t + 1].bitcast(U32), axis=0)),
           reads=[b_small], writes=[xg_bufs[par]], dma=True)
        emit_ln(xg, xg, [xg_bufs[par]], [xg_bufs[par]])
        if t == 0:
            tap("xg0", xg, [xg_bufs[par]])
        for j in range(DC):
            ph = j % 2
            pv = psv(2, cols=(ph * 512, ph * 512 + 128))
            em("tensor", lambda e, j=j, xg=xg, p=pv: e.transpose(
                p, xg[:, j * 128:(j + 1) * 128], IDENT),
               reads=[xg_bufs[par], b_consts], writes=[psb(2, ph)])
            em("scalar", lambda e, j=j, t=t, p=pv: e.copy(
                xgt_chunk(j)[:, t * 128:(t + 1) * 128], p),
               reads=[psb(2, ph)], writes=[b_xgt])

    segs = [(0, 512), (512, 1024), (1024, 1280)]
    for si, (s0, s1) in enumerate(segs):
        for c in range(DC):
            par = c % 2
            wk = AR[:, o_wkv + par * 1024:o_wkv + (par + 1) * 1024]
            em("sync", lambda e, c=c, wk=wk: e.dma_start(
                out=wk, in_=dt["wqkv"][c * 128:(c + 1) * 128, 1024:2048]),
               writes=[wkv_bufs[par]], dma=True)
            fns = [lambda e, g=g, c=c, wk=wk, s0=s0, s1=s1: e.matmul(
                psv(g // 2, cols=((g % 2) * 512, (g % 2) * 512 + s1 - s0)),
                wk[:, g * 128:(g + 1) * 128], xgt_chunk(c)[:, s0:s1],
                start=(c == 0), stop=(c == DC - 1)) for g in range(8)]
            em("tensor", fns, reads=[wkv_bufs[par], b_xgt],
               writes=[psb(g // 2, g % 2) for g in range(8)])
        for g in range(8):
            pv = psv(g // 2, cols=((g % 2) * 512, (g % 2) * 512 + s1 - s0))
            em("scalar", lambda e, g=g, s0=s0, s1=s1, pv=pv: e.activation(
                kt_chunk(g)[:, s0:s1], pv, AF.Identity, bias=BKC[:, g:g + 1]),
               reads=[psb(g // 2, g % 2), b_bqs], writes=[b_kt])

    waves = [[(t, v) for t in range(4) for v in range(2)],
             [(t, v) for t in range(4, 8) for v in range(2)],
             [(t, v) for t in range(8, 10) for v in range(2)]]
    for wave in waves:
        for c in range(DC):
            par = c % 2
            wv = AR[:, o_wkv + par * 1024:o_wkv + (par + 1) * 1024]
            em("sync", lambda e, c=c, wv=wv: e.dma_start(
                out=wv, in_=dt["wqkv"][c * 128:(c + 1) * 128, 2048:3072]),
               writes=[wkv_bufs[par]], dma=True)
            fns = [lambda e, t=t, v=v, c=c, wv=wv, gi=gi: e.matmul(
                psv(gi // 2, gi % 2),
                xgt_chunk(c)[:, t * 128:(t + 1) * 128],
                wv[:, v * 512:(v + 1) * 512],
                start=(c == 0), stop=(c == DC - 1))
                for gi, (t, v) in enumerate(wave)]
            em("tensor", fns, reads=[wkv_bufs[par], b_xgt],
               writes=[psb(gi // 2, gi % 2) for gi in range(len(wave))])
        for gi, (t, v) in enumerate(wave):
            pv = psv(gi // 2, gi % 2)
            em("vector", lambda e, t=t, v=v, pv=pv: e.tensor_add(
                v65_tile(t)[:, v * 520:v * 520 + 520]
                .rearrange("p (h d) -> p h d", d=65)[:, :, 0:64],
                pv.rearrange("p (h d) -> p h d", d=64),
                BQKVB[:, 2048 + v * 512:2048 + (v + 1) * 512]
                .rearrange("p (h d) -> p h d", d=64)),
               reads=[psb(gi // 2, gi % 2), b_bqkvb], writes=[b_v65])
    for t in range(GT):
        em("vector", lambda e, t=t: e.memset(
            v65_tile(t).rearrange("p (h d) -> p h d", d=65)[:, :, 64:65], 1.0),
           writes=[b_v65])
    tap("kt0", kt_chunk(0), [b_kt])
    tap("v650", v65_tile(0), [b_v65])
    tap("xgt0", xgt_chunk(0), [b_xgt])

    if PHASES < 4:
        return _finish()
    bld.barrier()
    ar.reset()
    o_wq = o_xgt
    b_wq = Buf("Wq")
    o_wp = ar.alloc(DC * 1024, "Wproj")
    b_wp = Buf("Wp")
    o_x4 = ar.alloc(2 * 1024, "x_tiles")
    x4_bufs = [Buf("x4a"), Buf("x4b")]
    o_xh4 = ar.alloc(1024, "xhat4")
    xh4_cb = [Buf(f"xh4_{j}") for j in range(DC)]
    o_qt = o_xgt + DC * 1024
    qt_bufs = [Buf("QT0"), Buf("QT1")]
    o_pt = ar.alloc(2 * GP, "PT")
    pt_bufs = [Buf("PT0"), Buf("PT1")]
    o_ot = ar.alloc(1024, "OT")
    b_ot = Buf("OT")
    o_osb = ar.alloc(160, "o_sb")
    osb_bufs = [Buf("osb0"), Buf("osb1")]
    b_rc = Buf("rc")
    b_x1s = Buf("x1s")

    OT = AR[:, o_ot:o_ot + 1024]
    XH4 = AR[:, o_xh4:o_xh4 + 1024]
    X1T = XH4

    for c in range(DC):
        em("sync", lambda e, c=c: e.dma_start(
            out=AR[:, o_wq + c * 1024:o_wq + (c + 1) * 1024],
            in_=dt["wqkv"][c * 128:(c + 1) * 128, 0:1024]),
           reads=[b_xgt], writes=[b_wq], dma=True)
        em("sync", lambda e, c=c: e.dma_start(
            out=AR[:, o_wp + c * 1024:o_wp + (c + 1) * 1024],
            in_=dt["wproj"][c * 128:(c + 1) * 128, :]),
           writes=[b_wp], dma=True)

    for ci in range(KCL):
        chunks = CLUSTER_CHUNKS[ci]
        nch = len(chunks)
        for qi in range(FC):
            tile = ci * FC + qi
            par = tile % 2
            xa = AR[:, o_x4 + par * 1024:o_x4 + (par + 1) * 1024]
            QT = AR[:, o_qt + par * 1024:o_qt + (par + 1) * 1024]
            b_qt = qt_bufs[par]
            em("sync", lambda e, t=tile, xa=xa: e.dma_start(
                out=xa, in_=dt["x"][t * 128:(t + 1) * 128, :]),
               writes=[x4_bufs[par]], dma=True)
            emit_ln(xa, XH4, [x4_bufs[par]], xh4_cb)
            for j in range(DC):
                pv = psv(3, cols=(0, 128))
                em("tensor", lambda e, j=j, p=pv: e.transpose(
                    p, XH4[:, j * 128:(j + 1) * 128], IDENT),
                   reads=[xh4_cb[j], b_consts], writes=[psb(3, 0)])
                em("scalar", lambda e, j=j, p=pv: e.copy(
                    XH4[:, j * 128:(j + 1) * 128], p),
                   reads=[psb(3, 0)], writes=[xh4_cb[j]])
            for qc in range(DC):
                pv = psv(3, cols=(0, 128))
                fns = [lambda e, qc=qc, c=c, pv=pv: e.matmul(
                    pv, AR[:, o_wq + c * 1024 + qc * 128:
                           o_wq + c * 1024 + (qc + 1) * 128],
                    XH4[:, c * 128:(c + 1) * 128],
                    start=(c == 0), stop=(c == DC - 1)) for c in range(DC)]
                em("tensor", fns, reads=[b_wq] + xh4_cb, writes=[psb(3, 0)])
                em("scalar", lambda e, qc=qc, pv=pv, QT=QT: e.activation(
                    QT[:, qc * 128:(qc + 1) * 128], pv, AF.Identity,
                    bias=BQC[:, qc:qc + 1], scale=SCALE),
                   reads=[psb(3, 0), b_bqs, b_consts], writes=[b_qt])
            if tile == 0:
                tap("qt0", QT, [b_qt])

            def emit_st(h, QT=QT, b_qt=b_qt, chunks=chunks, nch=nch, ci=ci):
                hp = h % 2
                PT = AR[:, o_pt + hp * GP:o_pt + (hp + 1) * GP]
                qrhs = QT[(h % 2) * 64:(h % 2) * 64 + 64,
                          (h // 2) * 128:(h // 2) * 128 + 128]
                groups = [chunks[i:i + 4] for i in range(0, nch, 4)]
                for gi, grp in enumerate(groups):
                    gh = gi % 2
                    fns = []
                    for j, (c, _) in enumerate(grp):
                        pv = psv(hp, cols=(gh * 512 + j * 128,
                                           gh * 512 + (j + 1) * 128))
                        kl = kt_chunk(h // 2)[(h % 2) * 64:(h % 2) * 64 + 64,
                                              c * 128:(c + 1) * 128]
                        fns.append(lambda e, pv=pv, kl=kl, q=qrhs: e.matmul(
                            pv, kl, q, start=True, stop=True))
                    em("tensor", fns, reads=[b_kt, b_qt],
                       writes=[psb(hp, gh)])
                    ncols = len(grp) * 128
                    em("scalar", lambda e, gi=gi, nc_=ncols, hp=hp, gh=gh,
                       PT=PT: e.activation(
                           PT[:, gi * 512:gi * 512 + nc_],
                           psv(hp, cols=(gh * 512, gh * 512 + nc_)), AF.Exp),
                       reads=[psb(hp, gh)], writes=[pt_bufs[hp]])
                fns = []
                for j, (c, _) in enumerate(chunks):
                    mc = mask_col(ci, c)
                    if mc is not None:
                        fns.append(lambda e, j=j, mc=mc, PT=PT:
                                   e.tensor_scalar_mul(
                            PT[:, j * 128:(j + 1) * 128],
                            PT[:, j * 128:(j + 1) * 128], mc))
                if fns:
                    em("vector", fns, reads=[b_consts], writes=[pt_bufs[hp]])

            def emit_pv(h, chunks=chunks, nch=nch, tile=tile):
                hp = h % 2
                PT = AR[:, o_pt + hp * GP:o_pt + (hp + 1) * GP]
                opv = psv(2, cols=(hp * 512, hp * 512 + 65))
                fns = [lambda e, j=j, c=c, h=h, opv=opv, PT=PT,
                       last=(j == nch - 1): e.matmul(
                    opv, PT[:, j * 128:(j + 1) * 128],
                    v65_tile(c)[:, h * 65:(h + 1) * 65],
                    start=(j == 0), stop=last)
                    for j, (c, _) in enumerate(chunks)]
                em("tensor", fns, reads=[pt_bufs[hp], b_v65],
                   writes=[psb(2, hp)])
                RC = AR[:, o_osb + 144 + hp:o_osb + 145 + hp]
                OSB = AR[:, o_osb + hp * 64:o_osb + (hp + 1) * 64]
                em("vector", lambda e, hp=hp, RC=RC: e.reciprocal(
                    RC, psv(2, cols=(hp * 512 + 64, hp * 512 + 65))),
                   reads=[psb(2, hp)], writes=[b_rc])
                em("vector", lambda e, hp=hp, RC=RC, OSB=OSB:
                   e.tensor_scalar_mul(OSB, psv(2, cols=(hp * 512,
                                                         hp * 512 + 64)), RC),
                   reads=[psb(2, hp), b_rc], writes=[osb_bufs[hp]])
                if tile == 0 and h == 0:
                    tap("pt0", PT, [pt_bufs[hp]])
                    tap("osb0", OSB, [osb_bufs[hp]])

            def emit_ot(h):
                hp = h % 2
                OSB = AR[:, o_osb + hp * 64:o_osb + (hp + 1) * 64]
                potv = PS[3][0:64, 512:640]
                em("tensor", lambda e, potv=potv, OSB=OSB: e.transpose(
                    potv, OSB, IDENT),
                   reads=[osb_bufs[hp], b_consts], writes=[psb(3, 1)])
                em("scalar", lambda e, h=h, potv=potv: e.copy(
                    OT[(h % 2) * 64:(h % 2) * 64 + 64,
                       (h // 2) * 128:(h // 2) * 128 + 128], potv),
                   reads=[psb(3, 1)], writes=[b_ot])

            for h in range(H):
                emit_st(h)
                if h >= 2:
                    emit_ot(h - 2)
                if h >= 1:
                    emit_pv(h - 1)
            emit_ot(H - 2)
            emit_pv(H - 1)
            emit_ot(H - 1)

            for dh in range(2):
                pv = psv(dh, cols=(0, 512))
                fns = [lambda e, c=c, dh=dh, pv=pv: e.matmul(
                    pv, OT[:, c * 128:(c + 1) * 128],
                    AR[:, o_wp + c * 1024 + dh * 512:
                       o_wp + c * 1024 + (dh + 1) * 512],
                    start=(c == 0), stop=(c == DC - 1)) for c in range(DC)]
                em("tensor", fns, reads=[b_ot, b_wp], writes=[psb(dh, 0)])
                em("vector", lambda e, dh=dh, pv=pv: e.tensor_add(
                    X1T[:, dh * 512:(dh + 1) * 512], pv,
                    BPROJB[:, dh * 512:(dh + 1) * 512]),
                   reads=[psb(dh, 0), b_bprojb],
                   writes=xh4_cb[dh * 4:(dh + 1) * 4])
                em("vector", lambda e, dh=dh, xa=xa: e.tensor_add(
                    X1T[:, dh * 512:(dh + 1) * 512],
                    X1T[:, dh * 512:(dh + 1) * 512],
                    xa[:, dh * 512:(dh + 1) * 512]),
                   reads=[x4_bufs[par]], writes=xh4_cb[dh * 4:(dh + 1) * 4])
            if tile == 0:
                tap("ot0", OT, [b_ot])
                tap("x1t0", X1T, xh4_cb)
            em("scalar", lambda e, t=tile: e.dma_start(
                out=dt["x1s"][t * 128:(t + 1) * 128, :], in_=X1T),
               reads=list(xh4_cb), writes=[b_x1s], dma=True)

    if PHASES < 5:
        return _finish()
    bld.barrier()
    ar.reset()
    o_tid = ar.alloc(3 * NT, "tid_tables")
    TIDG = AR[:, o_tid:o_tid + NT]
    TIDS = AR[:, o_tid + NT:o_tid + 2 * NT]
    GSLOT = AR[:, o_tid + 2 * NT:o_tid + 3 * NT]
    b_tid = Buf("tid")
    ar.mark()
    o_s8 = ar.alloc(5 * 4096, "slot8")
    MASKT = AR[0:8, o_s8:o_s8 + 4096]
    CUMT = AR[0:8, o_s8 + 4096:o_s8 + 8192]
    ZERO8 = AR[0:8, o_s8 + 8192:o_s8 + 12288]
    KEEPT = AR[0:8, o_s8 + 12288:o_s8 + 16384]
    PKT = AR[0:8, o_s8 + 16384:o_s8 + 20480]
    b_s8 = Buf("s8")
    o_r = ar.alloc(2 * 1024, "r_x1")
    r_bufs = [Buf("rx0"), Buf("rx1")]
    o_rxh = ar.alloc(1024, "r_xh")
    b_rxh = Buf("rxh")
    o_rxt = ar.alloc(1024, "r_xT")
    b_rxt = Buf("rxt")
    o_wg = ar.alloc(DC * 8, "wg")
    b_wg = Buf("wg")
    o_lg = ar.alloc(64, "logits")
    b_lg = Buf("lg")
    o_eg = ar.alloc(2 * NT, "eix_gate")
    b_eg = Buf("eg")
    o_ps5 = ar.alloc(16, "pair_slot")
    b_pair = Buf("pair")
    o_pairs = ar.alloc(4 * NT, "pairs_tmp")
    b_pairs = Buf("pairs")

    LG = AR[:, o_lg:o_lg + 8]
    NM = AR[:, o_lg + 8:o_lg + 9]
    RS = AR[:, o_lg + 9:o_lg + 10]
    RMAXP = AR[:, o_lg + 10:o_lg + 11]
    MSK = AR[:, o_lg + 16:o_lg + 24]
    EXPS = AR[:, o_lg + 24:o_lg + 32]
    KTOK = AR[:, o_lg + 32:o_lg + 33]
    PTOK = AR[:, o_lg + 33:o_lg + 34]
    EF = AR[:, o_lg + 34:o_lg + 35]
    BASE = AR[:, o_lg + 35:o_lg + 36]
    SLOTF = AR[:, o_lg + 36:o_lg + 37]
    SLOTU = AR[:, o_lg + 37:o_lg + 38]
    PM = AR[:, o_lg + 38:o_lg + 39]
    PD = AR[:, o_lg + 39:o_lg + 40]
    R1 = AR[:, o_lg + 40:o_lg + 41]
    MX8R = AR[:, o_lg + 48:o_lg + 56]
    IX8 = AR[:, o_lg + 56:o_lg + 64]
    EIX = AR[:, o_eg:o_eg + NT]
    GATE = AR[:, o_eg + NT:o_eg + 2 * NT]
    PAIR = AR[:, o_ps5:o_ps5 + 2]
    RXH = AR[:, o_rxh:o_rxh + 1024]
    PAIRS = AR[:, o_pairs:o_pairs + 2 * NT]
    TMPM = AR[:, o_pairs + 2 * NT:o_pairs + 3 * NT]
    TMP2 = AR[:, o_pairs + 3 * NT:o_pairs + 4 * NT]

    em("sync", lambda e: e.dma_start(
        out=AR[:, o_wg:o_wg + DC * 8].rearrange("p (c ee) -> p c ee", ee=8),
        in_=dt["wg"].rearrange("(c p) ee -> p c ee", p=128)),
       reads=[b_x1s], writes=[b_wg], dma=True)
    em("vector", lambda e: e.memset(ZERO8, 0.0), writes=[b_s8])
    b_slots_d = Buf("slots_dram")
    em("gpsimd", lambda e: e.dma_start(
        out=dt["slots"][0:N, :].rearrange("(p j) v -> p (j v)", p=128),
        in_=PREFILL),
       reads=[b_consts], writes=[b_slots_d], dma=True)

    for t in range(NT):
        par = t % 2
        xr = AR[:, o_r + par * 1024:o_r + (par + 1) * 1024]
        em("sync", lambda e, t=t, xr=xr: e.dma_start(
            out=xr, in_=dt["x1s"][t * 128:(t + 1) * 128, :]),
           reads=[b_x1s], writes=[r_bufs[par]], dma=True)
        emit_ln(xr, RXH, [r_bufs[par]], [b_rxh])
        emit_transposes(RXH, [b_rxh], o_rxt, [b_rxt])
        pv = psv(2, cols=(0, 8))
        fns = [lambda e, c=c, pv=pv: e.matmul(
            pv, AR[:, o_rxt + c * 128:o_rxt + (c + 1) * 128],
            AR[:, o_wg + c * 8:o_wg + (c + 1) * 8],
            start=(c == 0), stop=(c == DC - 1)) for c in range(DC)]
        em("tensor", fns, reads=[b_rxt, b_wg], writes=[psb(2, 0)])
        b_nm, b_rs, b_msk, b_ix8, b_mx8r = (
            Buf("nm"), Buf("rs"), Buf("msk"), Buf("ix8"), Buf("mx8r"))
        em("vector", lambda e, pv=pv: e.tensor_copy(LG, pv),
           reads=[psb(2, 0)], writes=[b_lg])
        em("vector", lambda e: e.max(out=MX8R, in_=LG),
           reads=[b_lg], writes=[b_mx8r])
        em("vector", lambda e: e.tensor_scalar_mul(NM, MX8R[:, 0:1], -1.0),
           reads=[b_mx8r], writes=[b_nm])
        em("scalar", lambda e: e.activation(EXPS, LG, AF.Exp,
                                            bias=NM, accum_out=RS),
           reads=[b_lg, b_nm], writes=[b_rs])
        em("vector", lambda e, t=t: e.reciprocal(GATE[:, t:t + 1], RS),
           reads=[b_rs], writes=[b_eg])
        em("vector", lambda e: e.max_index(out=IX8.bitcast(U32),
                                           in_max=MX8R, in_values=LG),
           reads=[b_lg, b_mx8r], writes=[b_ix8])
        em("vector", lambda e, t=t: e.tensor_copy(
            EIX[:, t:t + 1].bitcast(U32), IX8[:, 0:1].bitcast(U32)),
           reads=[b_ix8], writes=[b_eg])
        em("vector", lambda e: e.tensor_scalar(out=MSK, in0=LG, scalar1=NM,
                                               scalar2=0.0, op0=ALU.add,
                                               op1=ALU.is_equal),
           reads=[b_lg, b_nm], writes=[b_msk])
        em("tensor", lambda e: e.transpose(PS[3][0:8, 0:128], MSK, IDENT),
           reads=[b_msk, b_consts], writes=[psb(3, 0)])
        em("vector", lambda e, t=t: e.tensor_copy(
            MASKT[:, t * 128:(t + 1) * 128], PS[3][0:8, 0:128]),
           reads=[psb(3, 0)], writes=[b_s8])

    em("vector", lambda e: e.tensor_tensor_scan(
        out=CUMT, data0=MASKT, data1=ZERO8, initial=0.0,
        op0=ALU.add, op1=ALU.add), reads=[b_s8], writes=[b_s8])
    em("vector", lambda e: e.tensor_sub(CUMT, CUMT, MASKT),
       reads=[b_s8], writes=[b_s8])
    em("vector", lambda e: e.tensor_scalar(out=KEEPT, in0=CUMT,
                                           scalar1=float(CAP), scalar2=None,
                                           op0=ALU.is_lt),
       reads=[b_s8], writes=[b_s8])
    em("vector", lambda e: e.tensor_mul(KEEPT, KEEPT, MASKT),
       reads=[b_s8], writes=[b_s8])
    em("vector", lambda e: e.tensor_mul(PKT, CUMT, KEEPT),
       reads=[b_s8], writes=[b_s8])

    tap("maskt", MASKT, [b_s8])
    tap("cumt", CUMT, [b_s8])
    tap("keept", KEEPT, [b_s8])
    for t in range(NT):
        em("tensor", [
            lambda e, t=t: e.transpose(
                PS[3][0:128, 0:8], KEEPT[:, t * 128:(t + 1) * 128],
                IDENT[0:8, 0:8]),
            lambda e, t=t: e.transpose(
                PS[3][0:128, 16:24], PKT[:, t * 128:(t + 1) * 128],
                IDENT[0:8, 0:8]),
        ], reads=[b_s8, b_consts], writes=[psb(3, 0)])
        b_sl = Buf("sl")
        seq = [
            lambda e: e.reduce_sum(KTOK, PS[3][0:128, 0:8], axis=AX.X),
            lambda e: e.reduce_sum(PTOK, PS[3][0:128, 16:24], axis=AX.X),
            lambda e, t=t: e.tensor_copy(EF, EIX[:, t:t + 1].bitcast(U32)),
            lambda e: e.tensor_single_scalar(PD, PTOK, 128.0, ALU.is_ge),
            lambda e: e.tensor_single_scalar(PM, PTOK, 256.0, ALU.is_ge),
            lambda e: e.tensor_add(PD, PD, PM),
            lambda e: e.tensor_single_scalar(PM, PTOK, 384.0, ALU.is_ge),
            lambda e: e.tensor_add(PD, PD, PM),
            lambda e: e.tensor_scalar_mul(PM, PD, -4095.0),
            lambda e: e.scalar_tensor_tensor(out=R1, in0=PTOK, scalar=32.0,
                                             in1=PM, op0=ALU.mult, op1=ALU.add),
            lambda e: e.tensor_scalar(out=BASE, in0=EF, scalar1=4.0,
                                      scalar2=R1, op0=ALU.mult, op1=ALU.add),
            lambda e, t=t: e.tensor_sub(BASE, BASE, ALT[:, t:t + 1]),
            lambda e, t=t: e.scalar_tensor_tensor(
                out=SLOTF, in0=BASE, scalar=KTOK, in1=ALT[:, t:t + 1],
                op0=ALU.mult, op1=ALU.add),
            lambda e: e.tensor_copy(SLOTU.bitcast(U32), SLOTF),
            lambda e, t=t: e.tensor_copy(PAIR[:, 0:1], TOKID[:, t:t + 1]),
            lambda e, t=t: e.tensor_copy(PAIR[:, 1:2], GATE[:, t:t + 1]),
        ]
        for fi, f in enumerate(seq):
            em("vector", f, reads=[psb(3, 0), b_eg, b_consts, b_sl],
               writes=[b_pair if fi >= len(seq) - 3 else b_sl])
        em("gpsimd", lambda e: e.indirect_dma_start(
            out=dt["slots"][:, :],
            out_offset=IndirectOffsetOnAxis(ap=SLOTU.bitcast(U32), axis=0),
            in_=PAIR, in_offset=None),
           reads=[b_pair, b_slots_d], writes=[b_slots_d], dma=True)

    em("gpsimd", lambda e: e.dma_start(
        out=PAIRS,
        in_=dt["slots"][0:N, :].rearrange("(p j) v -> p (j v)", p=128)),
       reads=[b_slots_d], writes=[b_pairs], dma=True)
    PAIRS3 = PAIRS.rearrange("p (j v) -> p j v", v=2)
    b_rb = Buf("rb")
    for f in [
        lambda e: e.tensor_scalar_max(TMPM, PAIRS3[:, :, 0:1], 0.0),
        lambda e: e.tensor_copy(TIDG.bitcast(U32), TMPM),
        lambda e: e.tensor_copy(GSLOT, PAIRS3[:, :, 1:2]),
        lambda e: e.tensor_scalar(out=TMPM, in0=PAIRS3[:, :, 0:1], scalar1=0.0,
                                  scalar2=None, op0=ALU.is_ge),
        lambda e: e.tensor_sub(TMP2, PAIRS3[:, :, 0:1], ALTR),
        lambda e: e.tensor_mul(TMP2, TMP2, TMPM),
        lambda e: e.tensor_add(TMP2, TMP2, ALTR),
        lambda e: e.tensor_copy(TIDS.bitcast(U32), TMP2),
    ]:
        em("vector", f, reads=[b_pairs, b_consts, b_rb],
           writes=[b_tid, b_rb])

    tap("tidg", TIDG, [b_tid])
    tap("tids", TIDS, [b_tid])
    tap("gslot", GSLOT, [b_tid])
    tap("eix", EIX, [b_eg])
    tap("gate", GATE, [b_eg])

    if PHASES < 7:
        return _finish()
    bld.barrier()
    ar.reset()
    o_ht = ar.alloc(32 * 512, "hT")
    b_ht = Buf("hT")
    o_xe = ar.alloc(4 * 1024, "Xe")
    xe_bufs = [Buf(f"xe{j}") for j in range(4)]
    o_xet = ar.alloc(DC * 512, "XeT")
    b_xet = Buf("XeT")
    o_w1 = ar.alloc(2 * 1024, "w1buf")
    w1_bufs = [Buf("w1a"), Buf("w1b")]
    o_w2 = ar.alloc(2 * 1024, "w2buf")
    w2_bufs = [Buf("w2a"), Buf("w2b")]
    o_oe = ar.alloc(4 * 1024, "Oe")
    b_oe = Buf("Oe")

    def ht_chunk(i):
        return AR[:, o_ht + i * 512:o_ht + (i + 1) * 512]

    for ex in range(min(E, NEXPERTS)):
        for j in range(4):
            xe = AR[:, o_xe + j * 1024:o_xe + (j + 1) * 1024]
            em("gpsimd", lambda e, ex=ex, j=j, xe=xe: e.indirect_dma_start(
                out=xe, out_offset=None, in_=dt["x1s"][0:N, :],
                in_offset=IndirectOffsetOnAxis(
                    ap=TIDG[:, ex * 4 + j:ex * 4 + j + 1].bitcast(U32),
                    axis=0)),
               reads=[b_tid, b_x1s], writes=[xe_bufs[j]], dma=True)
            emit_ln(xe, xe, [xe_bufs[j]], [xe_bufs[j]])
            if P7STEPS < 2:
                continue
            for c in range(DC):
                ph = c % 2
                pv = psv(2, cols=(ph * 512, ph * 512 + 128))
                em("tensor", lambda e, c=c, xe=xe, pv=pv: e.transpose(
                    pv, xe[:, c * 128:(c + 1) * 128], IDENT),
                   reads=[xe_bufs[j], b_consts], writes=[psb(2, ph)])
                em("scalar", lambda e, c=c, j=j, pv=pv: e.copy(
                    AR[:, o_xet + c * 512 + j * 128:
                       o_xet + c * 512 + (j + 1) * 128], pv),
                   reads=[psb(2, ph)], writes=[b_xet])
        if P7STEPS < 3:
            continue
        for hfc in range(32):
            par = hfc % 2
            w1 = AR[:, o_w1 + par * 1024:o_w1 + (par + 1) * 1024]
            em("sync", lambda e, ex=ex, hfc=hfc, w1=w1: e.dma_start(
                out=w1, in_=dt["w1c"][ex, hfc]),
               writes=[w1_bufs[par]], dma=True)
            pv = psv(3, par)
            if FFN1_MODE != 1:
                fns = [lambda e, c=c, w1=w1, pv=pv: e.matmul(
                    pv, w1[:, c * 128:(c + 1) * 128],
                    AR[:, o_xet + c * 512:o_xet + (c + 1) * 512],
                    start=(c == 0), stop=(c == DC - 1)) for c in range(DC)]
                em("tensor", fns, reads=[w1_bufs[par], b_xet],
                   writes=[psb(3, par)])
            if FFN1_MODE == 2:
                em("vector", lambda e, hfc=hfc, pv=pv: e.tensor_copy(
                    ht_chunk(hfc), pv),
                   reads=[psb(3, par)], writes=[b_ht])
            elif FFN1_MODE == 0:
                em("scalar", lambda e, hfc=hfc, pv=pv: e.activation(
                    ht_chunk(hfc), pv, AF.Gelu_apprx_tanh),
                   reads=[psb(3, par)], writes=[b_ht])
        if P7STEPS < 4:
            continue
        for hfc in range(32):
            par = hfc % 2
            w2 = AR[:, o_w2 + par * 1024:o_w2 + (par + 1) * 1024]
            em("sync", lambda e, ex=ex, hfc=hfc, w2=w2: e.dma_start(
                out=w2, in_=dt["w2"][ex, hfc * 128:(hfc + 1) * 128, :]),
               writes=[w2_bufs[par]], dma=True)
            fns = [lambda e, g=g, hfc=hfc, w2=w2: e.matmul(
                psv(g // 2, g % 2),
                ht_chunk(hfc)[:, (g // 2) * 128:(g // 2 + 1) * 128],
                w2[:, (g % 2) * 512:(g % 2 + 1) * 512],
                start=(hfc == 0), stop=(hfc == 31)) for g in range(8)]
            em("tensor", fns, reads=[w2_bufs[par], b_ht],
               writes=[psb(g // 2, g % 2) for g in range(8)])
        for g in range(8):
            t, dh = g // 2, g % 2
            em("vector", lambda e, ex=ex, t=t, dh=dh, g=g: e.tensor_scalar_mul(
                AR[:, o_oe + t * 1024 + dh * 512:
                   o_oe + t * 1024 + (dh + 1) * 512],
                psv(g // 2, g % 2), GSLOT[:, ex * 4 + t:ex * 4 + t + 1]),
               reads=[psb(g // 2, g % 2), b_tid], writes=[b_oe])
        if ex == 0:
            tap("xet0", AR[:, o_xet:o_xet + 512], [b_xet])
            tap("ht0", ht_chunk(0), [b_ht])
            tap("oe0", AR[:, o_oe:o_oe + 1024], [b_oe])
        if P7STEPS < 5:
            continue
        for j in range(4):
            em("gpsimd", lambda e, ex=ex, j=j: e.indirect_dma_start(
                out=dt["x1s"][:, :],
                out_offset=IndirectOffsetOnAxis(
                    ap=TIDS[:, ex * 4 + j:ex * 4 + j + 1].bitcast(U32),
                    axis=0),
                in_=AR[:, o_oe + j * 1024:o_oe + (j + 1) * 1024],
                in_offset=None, compute_op=ALU.add),
               reads=[b_oe, b_tid], writes=[b_x1s], dma=True)

    if PHASES < 8:
        return _finish()
    b_out = Buf("out")
    for t in range(NT):
        em("sync", lambda e, t=t: e.dma_start(
            out=dt["out"][t * 128:(t + 1) * 128, :],
            in_=dt["x1s"][t * 128:(t + 1) * 128, :]),
           reads=[b_x1s], writes=[b_out], dma=True)
    return _finish()


_KF_ROWS = np.concatenate([np.arange(f * 128, (f + 1) * 128)
                           for f in (0, 8, 16, 24)])


def kernel(L0_attn_x, L0_attn_W_qkv, L0_attn_b_qkv, L0_attn_W_proj,
           L0_attn_b_proj, L0_attn_clusters, L0_attn_keyframes,
           L0_ffn_Wg, L0_ffn_W1, L0_ffn_W2):
    x = np.ascontiguousarray(np.asarray(L0_attn_x, np.float32))
    clusters = np.asarray(L0_attn_clusters)
    keyframes = np.asarray(L0_attn_keyframes)
    assert np.array_equal(
        clusters, np.arange(32, dtype=clusters.dtype).reshape(4, 8))
    assert np.array_equal(keyframes, clusters[:, 0])

    kfx = np.ascontiguousarray(x[:, _KF_ROWS, :].reshape(B * 512, D))
    w1 = np.ascontiguousarray(np.asarray(L0_ffn_W1, np.float32))
    w1c = np.ascontiguousarray(
        w1.reshape(E, DC, 128, 32, 128).transpose(0, 3, 2, 1, 4)
        .reshape(E, 32, 128, DC * 128))

    nc = build_program()
    shared = {
        "kfx": kfx,
        "wqkv": np.ascontiguousarray(np.asarray(L0_attn_W_qkv, np.float32)),
        "bqkv": np.ascontiguousarray(np.asarray(L0_attn_b_qkv, np.float32)),
        "wproj": np.ascontiguousarray(np.asarray(L0_attn_W_proj, np.float32)),
        "bproj": np.ascontiguousarray(np.asarray(L0_attn_b_proj, np.float32)),
        "wg": np.ascontiguousarray(np.asarray(L0_ffn_Wg, np.float32)),
        "w1c": w1c,
        "w2": np.ascontiguousarray(np.asarray(L0_ffn_W2, np.float32)),
        "consts": _make_consts(),
    }
    in_maps = [dict(shared, x=np.ascontiguousarray(x[b])) for b in range(B)]
    res = run_bass_kernel_spmd(nc, in_maps, core_ids=list(range(B)))
    return np.stack([res.results[b]["out"] for b in range(B)], axis=0)
